# version 10
# baseline (speedup 1.0000x reference)
import numpy as np
import concourse.bass as bass
import concourse.mybir as mybir
from concourse.bass_utils import run_bass_kernel_spmd

F32 = mybir.dt.float32
BF16 = mybir.dt.bfloat16
AF = mybir.ActivationFunctionType
ALU = mybir.AluOpType
AX = mybir.AxisListType

D = 1024
FC = 8
FF = 4096
E = 8
T_ALL = 4096
T_OWN = 2048
ST = 512
N_SST = T_ALL // ST
GT = 1024
NG = T_OWN // GT
EPS1 = 1e-5
EPS2 = 1e-6


class Rec:

    ENGS = ("t", "v", "s", "q", "y")

    def __init__(self, nc, stack):
        self.nc = nc
        self.stack = stack
        self.ops = {e: [] for e in self.ENGS}
        self.cnt = {e: 0 for e in self.ENGS}
        self.sems = {}
        self.dma_cnt = {}
        self.last_wait = {}
        for e in self.ENGS:
            self._getsem("__" + e)

    def _getsem(self, name):
        if name not in self.sems:
            self.sems[name] = self.stack.enter_context(self.nc.semaphore(name))
            self.dma_cnt[name] = 0
        return self.sems[name]

    def emit(self, eng, fn):
        self.cnt[eng] += 1
        self.ops[eng].append(("op", fn, "__" + eng, 1))
        return ("__" + eng, self.cnt[eng])

    def dma(self, eng, sem_name, fn):
        self._getsem(sem_name)
        self.dma_cnt[sem_name] += 16
        self.ops[eng].append(("dma", fn, sem_name, 16))
        return (sem_name, self.dma_cnt[sem_name])

    def wait(self, eng, token):
        if token is None:
            return
        if isinstance(token, tuple) and len(token) == 2 and \
                isinstance(token[0], tuple):
            for t_ in token:
                self.wait(eng, t_)
            return
        sem_name, val = token
        if sem_name == "__" + eng:
            return
        key = (eng, sem_name)
        if self.last_wait.get(key, -1) >= val:
            return
        self.last_wait[key] = val
        self.ops[eng].append(("wait", None, sem_name, val))

    def token(self, eng):
        return ("__" + eng, self.cnt[eng])

    def sync_self(self, eng, token):
        if token is None:
            return
        sem_name, val = token
        key = (eng, sem_name)
        if self.last_wait.get(key, -1) >= val:
            return
        self.last_wait[key] = val
        self.ops[eng].append(("wait", None, sem_name, val))

    def replay(self, block):
        def mk(eng_key):
            ser = eng_key in ("v",)
            own = self.sems["__" + eng_key]

            def body(e):
                done = 0
                for kind, fn, name, val in self.ops[eng_key]:
                    if kind == "wait":
                        e.wait_ge(self.sems[name], val)
                    else:
                        if ser and kind == "op" and done > 0:
                            e.wait_ge(own, done)
                        fn(e).then_inc(self.sems[name], val)
                        if kind == "op":
                            done += val
            return body
        block.tensor(mk("t"))
        block.vector(mk("v"))
        block.scalar(mk("s"))
        block.gpsimd(mk("q"))
        block.sync(mk("y"))


def build_nc(n_tiles=N_SST, n_groups=NG, n_experts=E, taps=None):
    from contextlib import ExitStack
    nc = bass.Bass("TRN2", target_bir_lowering=False)

    ssm_x = nc.dram_tensor("ssm_x", [T_ALL, D], F32, kind="ExternalInput").ap()
    sel = nc.dram_tensor("sel", [128, 1], F32, kind="ExternalInput").ap()
    Wd = nc.dram_tensor("Wd", [D, D], F32, kind="ExternalInput").ap()
    Wi = nc.dram_tensor("Wi", [D, D], F32, kind="ExternalInput").ap()
    bd = nc.dram_tensor("bd", [D], F32, kind="ExternalInput").ap()
    bi = nc.dram_tensor("bi", [D], F32, kind="ExternalInput").ap()
    agam = nc.dram_tensor("agam", [D], F32, kind="ExternalInput").ap()
    abet = nc.dram_tensor("abet", [D], F32, kind="ExternalInput").ap()
    rw = nc.dram_tensor("rw", [D, E], F32, kind="ExternalInput").ap()
    w1 = nc.dram_tensor("w1", [E, D, FF], F32, kind="ExternalInput").ap()
    b1 = nc.dram_tensor("b1", [E, FF], F32, kind="ExternalInput").ap()
    w2 = nc.dram_tensor("w2", [E, FF, D], F32, kind="ExternalInput").ap()
    b2 = nc.dram_tensor("b2", [E, D], F32, kind="ExternalInput").ap()
    fgam = nc.dram_tensor("fgam", [E, D], F32, kind="ExternalInput").ap()
    fbet = nc.dram_tensor("fbet", [E, D], F32, kind="ExternalInput").ap()
    out = nc.dram_tensor("out", [T_OWN, D], F32, kind="ExternalOutput").ap()
    xmid_dram = nc.dram_tensor("xmid_scr", [128, FC, T_OWN], F32, kind="Internal").ap()
    w1s_dram = nc.dram_tensor("w1s_scr", [E, 4, 128, FC * 1024], BF16, kind="Internal").ap()
    w2s_dram = nc.dram_tensor("w2s_scr", [E, 16, 128, 2 * 1024], BF16, kind="Internal").ap()
    ident_dram = nc.inline_tensor(np.eye(128, dtype=np.float32), "identm").ap()

    stack = ExitStack()
    r = Rec(nc, stack)
    eg = lambda t_: stack.enter_context(t_)
    taps = taps or []
    tap_set = set(taps)

    def tap(name, ap, tok):
        if name not in tap_set:
            return
        dt_ = ap.dtype
        dr = nc.dram_tensor("tap_" + name, list(ap.shape), dt_,
                            kind="ExternalOutput").ap()
        r.wait("y", tok)
        ttok = r.dma("y", "ost", lambda e, dr=dr, ap=ap: e.dma_start(dr, ap))
        for engk in ("t", "v", "s", "q", "y"):
            r.wait(engk, ttok)

    ident = eg(nc.sbuf_tensor("ident", [128, 128], F32))
    ones_col_b = eg(nc.sbuf_tensor("ones_col_b", [128, 1], BF16))
    ones_col_f = eg(nc.sbuf_tensor("ones_col_f", [128, 1], F32))
    ones_row_b = eg(nc.sbuf_tensor("ones_row_b", [1, 128], BF16))
    sel_c = eg(nc.sbuf_tensor("sel_c", [128, 1], F32))
    bd_c = eg(nc.sbuf_tensor("bd_c", [128, FC], F32))
    bi_c = eg(nc.sbuf_tensor("bi_c", [128, FC], F32))
    agam_c = eg(nc.sbuf_tensor("agam_c", [128, FC], F32))
    abet_c = eg(nc.sbuf_tensor("abet_c", [128, FC], F32))
    rw_b = eg(nc.sbuf_tensor("rw_b", [128, FC * E], BF16))
    b1_c = eg(nc.sbuf_tensor("b1_c", [128, E * 32], F32))
    b2_c = eg(nc.sbuf_tensor("b2_c", [128, E * FC], F32))
    fgam_c = eg(nc.sbuf_tensor("fgam_c", [128, E * FC], F32))
    fbet_c = eg(nc.sbuf_tensor("fbet_c", [128, E * FC], F32))
    eps_c = eg(nc.sbuf_tensor("eps_c", [1, 2], F32))
    rs_r = eg(nc.sbuf_tensor("rs_r", [1, GT], F32))
    nm_r = eg(nc.sbuf_tensor("nm_r", [1, GT], F32))
    rs_rb = eg(nc.sbuf_tensor("rs_rb", [1, GT], BF16))
    nm_rb = eg(nc.sbuf_tensor("nm_rb", [1, GT], BF16))
    g_r = eg(nc.sbuf_tensor("g_r", [1, GT], F32))
    u1_rb = eg(nc.sbuf_tensor("u1_rb", [1, GT], BF16))
    u2_rb = eg(nc.sbuf_tensor("u2_rb", [1, GT], BF16))
    g_rb = eg(nc.sbuf_tensor("g_rb", [1, GT], BF16))
    colscr = eg(nc.sbuf_tensor("colscr", [128, 32], F32))
    carry = eg(nc.sbuf_tensor("carry", [128, FC], F32))
    gates_tok = eg(nc.sbuf_tensor("gates_tok", [128, 64], F32))
    logit_tok = eg(nc.sbuf_tensor("logit_tok", [128, 64], F32))

    pall = eg(nc.psum_tensor("pall", [128, 4096], F32))

    def bank(i):
        return pall[:, i * 512:(i + 1) * 512]

    cl = "cst"
    r.dma("y", cl, lambda e: e.dma_start(ident[:], ident_dram))
    r.dma("y", cl, lambda e: e.dma_start(sel_c[:], sel))
    r.dma("y", cl, lambda e: e.dma_start(bd_c[:], bd.rearrange("(fc p) -> p fc", p=128), allow_slow_non_contiguous=True))
    r.dma("y", cl, lambda e: e.dma_start(bi_c[:], bi.rearrange("(fc p) -> p fc", p=128), allow_slow_non_contiguous=True))
    r.dma("y", cl, lambda e: e.dma_start(agam_c[:], agam.rearrange("(fc p) -> p fc", p=128), allow_slow_non_contiguous=True))
    r.dma("y", cl, lambda e: e.dma_start(abet_c[:], abet.rearrange("(fc p) -> p fc", p=128), allow_slow_non_contiguous=True))
    rwb_tok = r.dma("q", "cstq", lambda e: e.dma_start(
        rw_b[:].rearrange("p (fc e) -> p fc e", fc=FC),
        rw.rearrange("(fc p) e -> p fc e", p=128)))
    r.dma("y", cl, lambda e: e.dma_start(
        b1_c[:].rearrange("p (e m) -> p e m", e=E),
        b1.rearrange("e (m p) -> p e m", p=128),
        allow_slow_non_contiguous=True))
    r.dma("y", cl, lambda e: e.dma_start(
        b2_c[:].rearrange("p (e m) -> p e m", e=E),
        b2.rearrange("e (m p) -> p e m", p=128),
        allow_slow_non_contiguous=True))
    r.dma("y", cl, lambda e: e.dma_start(
        fgam_c[:].rearrange("p (e m) -> p e m", e=E),
        fgam.rearrange("e (m p) -> p e m", p=128),
        allow_slow_non_contiguous=True))
    r.dma("y", cl, lambda e: e.dma_start(
        fbet_c[:].rearrange("p (e m) -> p e m", e=E),
        fbet.rearrange("e (m p) -> p e m", p=128),
        allow_slow_non_contiguous=True))
    cst_tok = (cl, r.dma_cnt[cl])

    r.wait("v", cst_tok)
    r.emit("v", lambda e: e.memset(ones_col_b[:], 1.0 / D))
    r.emit("v", lambda e: e.memset(ones_col_f[:], 1.0 / D))
    r.emit("v", lambda e: e.memset(ones_row_b[:], 1.0))
    r.emit("v", lambda e: e.memset(carry[:], 0.0))
    r.emit("v", lambda e: e.memset(eps_c[0:1, 0:1], EPS1))
    r.emit("v", lambda e: e.memset(eps_c[0:1, 1:2], EPS2))
    ones_v = r.token("v")

    last_rows_v = [None]

    def rows_combine(ps_sum_row, ps_sq_row, eps, want_bf16, n=None,
                     rb_out=None, nb_out=None, deps=()):
        if n is None:
            n = ps_sum_row.shape[-1]
        if rb_out is None:
            rb_out = rs_rb
        if nb_out is None:
            nb_out = nm_rb
        for d_ in deps:
            r.wait("v", d_)
        r.wait("v", last_rows_v[0])
        r.emit("v", lambda e: e.tensor_copy(nm_r[0:1, 0:n], ps_sum_row))
        r.emit("v", lambda e: e.tensor_tensor(
            nm_r[0:1, 0:n], nm_r[0:1, 0:n], ps_sum_row, ALU.mult))
        r.emit("v", lambda e: e.tensor_tensor(
            rs_r[0:1, 0:n], ps_sq_row, nm_r[0:1, 0:n], ALU.subtract))
        vtok = r.token("v")
        r.wait("s", vtok)
        ei = 0 if eps == EPS1 else 1
        r.emit("s", lambda e, ei=ei: e.activation(
            rs_r[0:1, 0:n], rs_r[0:1, 0:n], AF.Sqrt, bias=eps_c[0:1, ei:ei + 1]))
        stok = r.token("s")
        r.wait("v", stok)
        r.emit("v", lambda e: e.reciprocal(rs_r[0:1, 0:n], rs_r[0:1, 0:n]))
        r.emit("v", lambda e: e.scalar_tensor_tensor(
            nm_r[0:1, 0:n], ps_sum_row, -1.0, rs_r[0:1, 0:n],
            ALU.mult, ALU.mult))
        rd_tok = r.token("v")
        if want_bf16:
            r.emit("v", lambda e: e.tensor_copy(rb_out[0:1, 0:n], rs_r[0:1, 0:n]))
            r.emit("v", lambda e: e.tensor_copy(nb_out[0:1, 0:n], nm_r[0:1, 0:n]))
        last_rows_v[0] = r.token("v")
        return rd_tok, last_rows_v[0]

    ssm_scope = ExitStack()
    es = lambda t_: ssm_scope.enter_context(t_)
    WdWi_b = es(nc.sbuf_tensor("WdWi_b", [128, 2 * FC * 1024], BF16))
    xt = [es(nc.sbuf_tensor(f"xt{i}", [128, 4 * 1024], F32)) for i in range(2)]
    x_f2 = [es(nc.sbuf_tensor(f"x_f{i}", [128, FC * ST], F32)) for i in range(2)]
    xn_b = es(nc.sbuf_tensor("xn_b", [128, FC * ST], BF16))
    dec = es(nc.sbuf_tensor("dec", [128, FC * ST], F32))
    vscr = es(nc.sbuf_tensor("vscr", [128, 512], F32))
    inj = es(nc.sbuf_tensor("inj", [128, FC * ST], F32))
    states = es(nc.sbuf_tensor("states", [128, FC * ST], F32))
    ps_rs_rb = es(nc.sbuf_tensor("ps_rs_rb", [1, ST], BF16))
    ps_nm_rb = es(nc.sbuf_tensor("ps_nm_rb", [1, ST], BF16))
    st_sq = dec[:].bitcast(BF16)[:, 0:FC * ST]
    st_sq2 = inj[:].bitcast(BF16)[:, 0:FC * ST]

    wsem = "wdi"
    r.dma("q", wsem, lambda e: e.dma_start(
        WdWi_b[:, 0:FC * 1024].rearrange("p (kk d) -> p kk d", kk=FC),
        Wd.rearrange("(kk p) d -> p kk d", p=128)))
    r.dma("q", wsem, lambda e: e.dma_start(
        WdWi_b[:, FC * 1024:].rearrange("p (kk d) -> p kk d", kk=FC),
        Wi.rearrange("(kk p) d -> p kk d", p=128)))
    wdi_tok = (wsem, r.dma_cnt[wsem])

    xt_tok = [None] * (N_SST + 2)
    xt_free = [None, None]
    bank_free = [None] * 8
    tk = {}
    xm_store = [None] * (N_SST // 2)
    dec_tok = [None] * FC
    inj_tok = [None] * FC

    def load_xt(k):
        par = k % 2
        r.wait("y", xt_free[par])
        xt_tok[k] = r.dma("y", f"xl{par}", lambda e, k=k, par=par: e.dma_start(
            xt[par][:].rearrange("p (j d) -> p j d", j=4),
            ssm_x[k * ST:(k + 1) * ST, :].rearrange("(j p) d -> p j d", p=128)))

    load_xt(0)
    load_xt(1)

    def tp_evac(k):
        par = k % 2
        xf = x_f2[par]
        r.wait("t", cst_tok)
        r.wait("t", ones_v)
        r.wait("t", xt_tok[k])
        for fc in range(FC):
            b_ = fc % 2
            r.wait("t", bank_free[b_])
            for j in range(4):
                src2 = xt[par][:, j * 1024 + fc * 128: j * 1024 + (fc + 1) * 128]
                dst = pall[:, b_ * 512 + j * 128: b_ * 512 + (j + 1) * 128]
                r.emit("t", lambda e, d=dst, s2=src2: e.transpose(d, s2, ident[:]))
            tp_t = r.token("t")
            r.wait("s", tp_t)
            if fc == 0:
                for t_ in tk.get(("xfg", par), []):
                    r.wait("s", t_)
            r.emit("s", lambda e, fc=fc, b_=b_, xf=xf: e.activation(
                xf[:, fc * ST:(fc + 1) * ST], bank(b_), AF.Copy))
            bank_free[b_] = r.token("s")
        tk[("evac", k)] = r.token("s")
        xt_free[par] = r.token("t")
        if k + 2 < n_tiles:
            load_xt(k + 2)

    def stats_sum(k):
        xf = x_f2[k % 2]
        r.wait("t", tk[("evac", k)])
        r.wait("t", bank_free[6])
        for fc in range(FC):
            rhs = xf[:, fc * ST:(fc + 1) * ST]
            r.emit("t", lambda e, rr=rhs, fc=fc: e.matmul(
                pall[0:1, 6 * 512:7 * 512], ones_col_f[:], rr,
                start=(fc == 0), stop=(fc == FC - 1)))
        tk[("summ", k)] = r.token("t")

    def square(k):
        xf = x_f2[k % 2]
        r.wait("s", tk.get(("scanall", k - 1)))
        r.wait("s", tk.get(("sqmm", k - 1)))
        r.sync_self("s", tk[("evac", k)])
        r.emit("s", lambda e, xf=xf: e.activation(st_sq, xf[:], AF.Square))
        tk[("sq", k)] = r.token("s")

    def stats_sq(k):
        r.wait("t", tk[("sq", k)])
        r.wait("t", bank_free[7])
        for fc in range(FC):
            rhs = st_sq[:, fc * ST:(fc + 1) * ST]
            r.emit("t", lambda e, rr=rhs, fc=fc: e.matmul(
                pall[0:1, 7 * 512:8 * 512], ones_col_b[:], rr,
                start=(fc == 0), stop=(fc == FC - 1)))
        tk[("sqmm", k)] = r.token("t")

    def combine1(k):
        rd, done = rows_combine(pall[0:1, 6 * 512:7 * 512],
                                pall[0:1, 7 * 512:8 * 512], EPS1, True, ST,
                                deps=[tk[("summ", k)], tk[("sqmm", k)],
                                      tk.get(("bc1", k - 1))])
        bank_free[6] = rd
        bank_free[7] = rd
        tk[("comb", k)] = done

    def bc1(k):
        r.wait("t", tk[("comb", k)])
        r.wait("t", bank_free[6])
        r.emit("t", lambda e: e.matmul(
            bank(6), ones_row_b[:], rs_rb[0:1, 0:ST], start=True, stop=True))
        r.wait("t", bank_free[7])
        r.emit("t", lambda e: e.matmul(
            bank(7), ones_row_b[:], nm_rb[0:1, 0:ST], start=True, stop=True))
        tk[("bc1", k)] = r.token("t")

    def norm(k):
        own = k >= N_SST // 2
        par = k % 2
        xf = x_f2[par]
        r.wait("v", tk[("bc1", k)])
        r.wait("v", tk.get(("decinj", k - 1)))
        for fc in range(FC):
            sl = slice(fc * ST, (fc + 1) * ST)
            r.emit("v", lambda e, sl=sl, xf=xf: e.tensor_tensor(
                vscr[:], xf[:, sl], bank(6), ALU.mult))
            r.emit("v", lambda e, sl=sl: e.tensor_tensor(
                xn_b[:, sl], vscr[:], bank(7), ALU.add))
            if own:
                r.emit("v", lambda e, sl=sl, xf=xf: e.tensor_tensor(
                    xf[:, sl], xf[:, sl], vscr[:], ALU.add))
                r.emit("v", lambda e, sl=sl, xf=xf: e.tensor_tensor(
                    xf[:, sl], xf[:, sl], bank(7), ALU.add))
        v = r.token("v")
        tk[("norm", k)] = v
        bank_free[6] = v
        bank_free[7] = v
        if not own:
            tk[("xfg", par)] = [v, tk[("summ", k)]]

    def decinj(k):
        r.wait("t", wdi_tok)
        r.wait("t", tk[("norm", k)])
        mm_banks = (4, 5)
        bi_ = 0
        for which in range(2):
            for m in range(FC):
                b_ = mm_banks[bi_ % 2]
                bi_ += 1
                r.wait("t", bank_free[b_])
                for kk in range(FC):
                    lhs = WdWi_b[:, which * FC * 1024 + kk * 1024 + m * 128:
                                 which * FC * 1024 + kk * 1024 + (m + 1) * 128]
                    rhs = xn_b[:, kk * ST:(kk + 1) * ST]
                    r.emit("t", lambda e, l=lhs, rr=rhs, b_=b_, kk=kk: e.matmul(
                        bank(b_), l, rr, start=(kk == 0), stop=(kk == FC - 1)))
                mm_t = r.token("t")
                r.wait("s", mm_t)
                if which == 0 and m == 0:
                    r.wait("s", tk[("sqmm", k)])
                if which == 1 and m == 0:
                    r.wait("s", tk.get(("postmm", k - 1)))
                if which == 0:
                    r.emit("s", lambda e, m=m, b_=b_: e.activation(
                        dec[:, m * ST:(m + 1) * ST], bank(b_), AF.Sigmoid,
                        bias=bd_c[:, m:m + 1]))
                    dec_tok[m] = r.token("s")
                else:
                    r.emit("s", lambda e, m=m, b_=b_: e.activation(
                        inj[:, m * ST:(m + 1) * ST], bank(b_), AF.Identity,
                        bias=bi_c[:, m:m + 1]))
                    inj_tok[m] = r.token("s")
                bank_free[b_] = r.token("s")
        tk[("decinj", k)] = r.token("t")

    def scans(k):
        if k == N_SST // 2:
            r.emit("v", lambda e: e.tensor_scalar(
                carry[:], carry[:], sel_c[:, 0:1], None, ALU.mult))
        r.wait("v", tk.get(("postmm", k - 1)))
        r.wait("v", tk.get(("psq", k - 1)))
        for fc in range(FC):
            r.wait("v", dec_tok[fc])
            r.wait("v", inj_tok[fc])
            r.emit("v", lambda e, fc=fc: e.tensor_tensor_scan(
                states[:, fc * ST:(fc + 1) * ST], dec[:, fc * ST:(fc + 1) * ST],
                inj[:, fc * ST:(fc + 1) * ST], carry[:, fc:fc + 1],
                ALU.mult, ALU.add))
        r.emit("v", lambda e: e.tensor_copy(carry[:], states[:, ST - 1::ST]))
        tk[("scanall", k)] = r.token("v")

    def post_stats(k):
        r.wait("s", tk[("scanall", k)])
        r.emit("s", lambda e: e.activation(st_sq2, states[:], AF.Square))
        tk[("psq", k)] = r.token("s")
        r.wait("t", bank_free[2])
        r.wait("t", bank_free[3])
        r.wait("t", tk[("scanall", k)])
        r.wait("t", tk[("psq", k)])
        for fc in range(FC):
            rhs = states[:, fc * ST:(fc + 1) * ST]
            r.emit("t", lambda e, rr=rhs, fc=fc: e.matmul(
                pall[0:1, 2 * 512:3 * 512], ones_col_f[:], rr,
                start=(fc == 0), stop=(fc == FC - 1)))
        for fc in range(FC):
            rhs = st_sq2[:, fc * ST:(fc + 1) * ST]
            r.emit("t", lambda e, rr=rhs, fc=fc: e.matmul(
                pall[0:1, 3 * 512:4 * 512], ones_col_b[:], rr,
                start=(fc == 0), stop=(fc == FC - 1)))
        tk[("postmm", k)] = r.token("t")

    def post_combine(k):
        rd, done = rows_combine(pall[0:1, 2 * 512:3 * 512],
                                pall[0:1, 3 * 512:4 * 512], EPS1, True, ST,
                                rb_out=ps_rs_rb, nb_out=ps_nm_rb,
                                deps=[tk[("postmm", k)], tk.get(("bc2", k - 1))])
        bank_free[2] = rd
        bank_free[3] = rd
        tk[("pcomb", k)] = done

    def bc2(k):
        r.wait("t", tk[("pcomb", k)])
        r.wait("t", bank_free[2])
        r.emit("t", lambda e: e.matmul(
            bank(2), ones_row_b[:], ps_rs_rb[0:1, 0:ST], start=True, stop=True))
        r.wait("t", bank_free[3])
        r.emit("t", lambda e: e.matmul(
            bank(3), ones_row_b[:], ps_nm_rb[0:1, 0:ST], start=True, stop=True))
        tk[("bc2", k)] = r.token("t")

    def apply_store(k):
        par = k % 2
        xf = x_f2[par]
        r.wait("v", tk[("bc2", k)])
        for fc in range(FC):
            sl = slice(fc * ST, (fc + 1) * ST)
            r.emit("v", lambda e, sl=sl: e.tensor_tensor(
                vscr[:], states[:, sl], bank(2), ALU.mult))
            r.emit("v", lambda e: e.tensor_tensor(
                vscr[:], vscr[:], bank(3), ALU.add))
            r.emit("v", lambda e, sl=sl, fc=fc, xf=xf: e.scalar_tensor_tensor(
                xf[:, sl], vscr[:], agam_c[:, fc:fc + 1], xf[:, sl],
                ALU.mult, ALU.add))
            sttv = r.token("v")
            r.wait("s", sttv)
            r.emit("s", lambda e, sl=sl, fc=fc, xf=xf: e.activation(
                xf[:, sl], xf[:, sl], AF.Identity, bias=abet_c[:, fc:fc + 1]))
        apply_v = r.token("v")
        beta_s = r.token("s")
        bank_free[2] = apply_v
        bank_free[3] = apply_v
        tap(f"xmid{k}", xf[:], beta_s)
        kk_own = k - N_SST // 2
        r.wait("y", beta_s)
        st = r.dma("y", f"xms{kk_own % 2}", lambda e, kk_own=kk_own, xf=xf:
                   e.dma_start(xmid_dram[:, :, kk_own * ST:(kk_own + 1) * ST],
                               xf[:].rearrange("p (fc t) -> p fc t", fc=FC)))
        xm_store[kk_own] = st
        tk[("xfg", par)] = [st]

    tp_evac(0)
    stats_sum(0)
    square(0)
    stats_sq(0)
    combine1(0)
    bc1(0)
    norm(0)
    for k in range(n_tiles):
        own = k >= N_SST // 2
        decinj(k)
        scans(k)
        if own:
            post_stats(k)
        if k + 1 < n_tiles:
            tp_evac(k + 1)
            stats_sum(k + 1)
            square(k + 1)
            stats_sq(k + 1)
        if own:
            post_combine(k)
        if k + 1 < n_tiles:
            combine1(k + 1)
        if own:
            bc2(k)
        if k + 1 < n_tiles:
            bc1(k + 1)
            norm(k + 1)
        if own:
            apply_store(k)
        tap(f"xn_b{k}", xn_b[:], tk[("decinj", k)])
        tap(f"states{k}", states[:], tk[("scanall", k)])

    xms_all = [("xms0", r.dma_cnt.get("xms0", 0)), ("xms1", r.dma_cnt.get("xms1", 0))]
    ssm_t_done = r.token("t")
    ssm_v_done = r.token("v")
    ssm_s_done = r.token("s")
    ssm_scope.close()

    ffn_scope = ExitStack()
    ef = lambda t_: ffn_scope.enter_context(t_)
    h_b = ef(nc.sbuf_tensor("h_b", [128, 32 * 1024], BF16))
    xn2_b = ef(nc.sbuf_tensor("xn2_b", [128, FC * GT], BF16))
    yacc = ef(nc.sbuf_tensor("yacc", [128, FC * GT], F32))
    res_b = ef(nc.sbuf_tensor("res_b", [128, FC * GT], BF16))
    sq_b = ef(nc.sbuf_tensor("sq_b", [128, FC * GT // 4], BF16))
    w1buf0 = ef(nc.sbuf_tensor("w1buf0", [128, FC * 1024], BF16))
    w1buf1 = ef(nc.sbuf_tensor("w1buf1", [128, FC * 1024], BF16))
    w2buf = ef(nc.sbuf_tensor("w2buf", [128, 3 * 2048], BF16))
    tscr = sq_b[:].bitcast(F32)[:, 0:GT]
    out_tok = [h_b[:].bitcast(F32)[:, i * D:(i + 1) * D] for i in range(2)]
    w1bufs = (w1buf0, w1buf1)

    for engk in ("t", "v", "s", "y", "q"):
        r.wait(engk, ssm_v_done)
        r.wait(engk, ssm_s_done)
        r.wait(engk, ssm_t_done)

    w1_round = [0]
    w1_free = [None, None]
    w1_slot_store = [None, None]
    w2_slot_store = [None, None, None]
    w1s_tok = {}
    w2s_tok = {}

    def prefetch_w1(ee, ms, g):
        par = w1_round[0] % 2
        r.wait("q", w1_free[par])
        r.wait("y", w1_free[par])
        r.wait("q", w1_slot_store[par])
        r.wait("y", w1_slot_store[par])
        if g == 0:
            tok = r.dma("q", f"w1{par}", lambda e, ee=ee, ms=ms, par=par: e.dma_start(
                w1bufs[par][:].rearrange("p (kk f) -> p kk f", kk=FC),
                w1[ee, :, ms * 1024:(ms + 1) * 1024].rearrange(
                    "(kk p) f -> p kk f", p=128)))
            r.wait("y", tok)
            stok = r.dma("y", f"w1s{par}", lambda e, ee=ee, ms=ms, par=par:
                         e.dma_start(w1s_dram[ee, ms], w1bufs[par][:]))
            w1s_tok[(ee, ms)] = stok
            w1_slot_store[par] = stok
        else:
            r.wait("y", w1s_tok[(ee, ms)])
            tok = r.dma("y", f"w1h{par}", lambda e, ee=ee, ms=ms, par=par:
                        e.dma_start(w1bufs[par][:], w1s_dram[ee, ms]))
        w1_round[0] += 1
        return tok, par

    w2_round = [0]
    w2_free = [None, None, None]

    def prefetch_w2(ee, kb, g, th):
        par = w2_round[0] % 3
        r.wait("q", w2_free[par])
        r.wait("y", w2_free[par])
        r.wait("q", w2_slot_store[par])
        r.wait("y", w2_slot_store[par])
        dst = w2buf[:, par * 2048:(par + 1) * 2048]
        if g == 0 and th == 0:
            tok = r.dma("q", f"w2{par}", lambda e, ee=ee, kb=kb, dst=dst: e.dma_start(
                dst.rearrange("p (kk d) -> p kk d", kk=2),
                w2[ee, kb * 256:(kb + 1) * 256, :].rearrange(
                    "(kk p) d -> p kk d", p=128)))
            r.wait("y", tok)
            stok = r.dma("y", f"w2s{par}", lambda e, ee=ee, kb=kb, dst=dst:
                         e.dma_start(w2s_dram[ee, kb], dst))
            w2s_tok[(ee, kb)] = stok
            w2_slot_store[par] = stok
        else:
            r.wait("y", w2s_tok[(ee, kb)])
            tok = r.dma("y", f"w2h{par}", lambda e, ee=ee, kb=kb, dst=dst:
                        e.dma_start(dst, w2s_dram[ee, kb]))
        w2_round[0] += 1
        return tok, par


    prev_a_mm_tok = None
    prev_ost_all = None
    prev_out_tp = None
    prev_gr_t = None
    res_read_tok = None
    prev_est_t = None
    prev_ebc_t = None
    w1tok = w1par = None
    w2tok = w2par = None

    for g in range(n_groups):
        for t_ in xms_all:
            r.wait("y", t_)
        r.wait("y", prev_out_tp)
        yld = r.dma("y", "yld", lambda e, g=g: e.dma_start(
            yacc[:].rearrange("p (fc t) -> p fc t", fc=FC),
            xmid_dram[:, :, g * GT:(g + 1) * GT]))

        r.wait("v", yld)
        r.wait("v", prev_est_t)
        if prev_ost_all is not None:
            for t_ in prev_ost_all:
                r.wait("s", t_)
        r.emit("v", lambda e: e.tensor_copy(res_b[:], yacc[:]))
        xmb_v = r.token("v")
        for b_ in range(4):
            r.wait("t", bank_free[b_])
        r.wait("t", xmb_v)
        for nt in range(2):
            for fc in range(FC):
                rhs = res_b[:, fc * GT + nt * 512: fc * GT + nt * 512 + 512]
                r.emit("t", lambda e, rr=rhs, fc=fc, nt=nt: e.matmul(
                    pall[0:1, nt * 512: nt * 512 + 512],
                    ones_col_b[:], rr, start=(fc == 0), stop=(fc == FC - 1)))
        for half in range(4):
            r.wait("s", xmb_v)
            if half > 0:
                r.wait("s", r.token("t"))
            r.emit("s", lambda e, half=half: e.activation(
                sq_b[:], res_b[:, half * 2 * GT:(half + 1) * 2 * GT], AF.Square))
            sqh_s = r.token("s")
            r.wait("t", sqh_s)
            for nt in range(2):
                for fc in range(2):
                    rhs = sq_b[:, fc * GT + nt * 512: fc * GT + nt * 512 + 512]
                    fcg = half * 2 + fc
                    r.emit("t", lambda e, rr=rhs, fcg=fcg, nt=nt: e.matmul(
                        pall[0:1, 1024 + nt * 512: 1024 + nt * 512 + 512],
                        ones_col_b[:], rr,
                        start=(fcg == 0), stop=(fcg == FC - 1)))
        ln2_mm_t = r.token("t")
        ps_sum_row = pall[0:1, 0:1024]
        ps_sq_row = pall[0:1, 1024:2048]
        rowsdone = rows_combine(ps_sum_row, ps_sq_row, EPS1, True,
                                deps=[ln2_mm_t])
        for b_ in range(4):
            bank_free[b_] = rowsdone[0]
        rows2_v = rowsdone[1]
        r.wait("t", rows2_v)
        for nt in range(2):
            r.wait("t", bank_free[nt])
            r.emit("t", lambda e, nt=nt: e.matmul(
                bank(nt), ones_row_b[:], rs_rb[0:1, nt * 512:(nt + 1) * 512],
                start=True, stop=True))
            r.wait("t", bank_free[2 + nt])
            r.emit("t", lambda e, nt=nt: e.matmul(
                bank(2 + nt), ones_row_b[:], nm_rb[0:1, nt * 512:(nt + 1) * 512],
                start=True, stop=True))
        ln2_bc_t = r.token("t")
        r.wait("v", ln2_bc_t)
        r.wait("v", prev_a_mm_tok)
        for fc in range(FC):
            for nt in range(2):
                sl = slice(fc * GT + nt * 512, fc * GT + nt * 512 + 512)
                r.emit("v", lambda e, sl=sl, nt=nt: e.tensor_tensor(
                    tscr[:, 0:512], yacc[:, sl], bank(nt), ALU.mult))
                r.emit("v", lambda e, sl=sl, nt=nt: e.tensor_tensor(
                    xn2_b[:, sl], tscr[:, 0:512], bank(2 + nt), ALU.add))
        xn2_v = r.token("v")
        for b_ in range(4):
            bank_free[b_] = xn2_v

        rt_bank = 4
        r.wait("t", xn2_v)
        r.wait("t", rwb_tok)
        r.wait("t", bank_free[rt_bank])
        for tc in range(FC):
            for fc in range(FC):
                lhs = xn2_b[:, fc * GT + tc * 128: fc * GT + (tc + 1) * 128]
                r.emit("t", lambda e, l=lhs, tc=tc, fc=fc, rt_bank=rt_bank: e.matmul(
                    pall[:, rt_bank * 512 + tc * 8: rt_bank * 512 + (tc + 1) * 8],
                    l, rw_b[:, fc * E:(fc + 1) * E],
                    start=(fc == 0), stop=(fc == FC - 1)))
        rt_t = r.token("t")
        r.wait("v", rt_t)
        r.wait("v", prev_gr_t)
        r.emit("v", lambda e, rt_bank=rt_bank: e.tensor_copy(
            logit_tok[:], pall[:, rt_bank * 512: rt_bank * 512 + 64]))
        bank_free[rt_bank] = r.token("v")
        for tc in range(FC):
            lt = logit_tok[:, tc * 8:(tc + 1) * 8]
            r.emit("v", lambda e, lt=lt: e.tensor_tensor(
                colscr[:, 4:8], lt[:, 0:4], lt[:, 4:8], ALU.max))
            r.emit("v", lambda e: e.tensor_tensor(
                colscr[:, 8:10], colscr[:, 4:6], colscr[:, 6:8], ALU.max))
            r.emit("v", lambda e: e.tensor_tensor(
                colscr[:, 0:1], colscr[:, 8:9], colscr[:, 9:10], ALU.max))
            r.emit("v", lambda e, lt=lt: e.tensor_scalar(
                colscr[:, 4:12], lt, colscr[:, 0:1], None, ALU.is_ge))
            r.emit("v", lambda e, lt=lt: e.scalar_tensor_tensor(
                colscr[:, 16:24], colscr[:, 4:12], -1e30, lt, ALU.mult, ALU.add))
            r.emit("v", lambda e: e.tensor_tensor(
                colscr[:, 24:28], colscr[:, 16:20], colscr[:, 20:24], ALU.max))
            r.emit("v", lambda e: e.tensor_tensor(
                colscr[:, 28:30], colscr[:, 24:26], colscr[:, 26:28], ALU.max))
            r.emit("v", lambda e: e.tensor_tensor(
                colscr[:, 1:2], colscr[:, 28:29], colscr[:, 29:30], ALU.max))
            r.emit("v", lambda e: e.tensor_scalar(
                colscr[:, 16:24], colscr[:, 16:24], colscr[:, 1:2], None, ALU.is_ge))
            r.emit("v", lambda e: e.tensor_tensor(
                colscr[:, 2:3], colscr[:, 0:1], colscr[:, 1:2], ALU.subtract))
            r.emit("v", lambda e: e.tensor_scalar(
                colscr[:, 3:4], colscr[:, 2:3], -1.0, None, ALU.mult))
            sg_v = r.token("v")
            r.wait("s", sg_v)
            r.emit("s", lambda e: e.activation(
                colscr[:, 12:14], colscr[:, 2:4], AF.Sigmoid))
            sig_s = r.token("s")
            r.wait("v", sig_s)
            r.emit("v", lambda e, tc=tc: e.tensor_scalar(
                gates_tok[:, tc * 8:(tc + 1) * 8], colscr[:, 4:12],
                colscr[:, 12:13], None, ALU.mult))
            r.emit("v", lambda e, tc=tc: e.scalar_tensor_tensor(
                gates_tok[:, tc * 8:(tc + 1) * 8], colscr[:, 16:24],
                colscr[:, 13:14], gates_tok[:, tc * 8:(tc + 1) * 8],
                ALU.mult, ALU.add))
        gates_v = r.token("v")

        if w1tok is None:
            w1tok, w1par = prefetch_w1(0, 0, g)
        prev_y_v = None
        tap(f"yld{g}", yacc[:], ("__v", r.cnt["v"]))
        tap(f"xn2_b{g}", xn2_b[:], xn2_v)
        tap(f"logit{g}", logit_tok[:], gates_v)
        tap(f"gates{g}", gates_tok[:], gates_v)

        def emit_A_ms(ee, ms):
            nonlocal w1tok, w1par
            cur_tok, cur_par = w1tok, w1par
            if ms < 3:
                w1tok, w1par = prefetch_w1(ee, ms + 1, g)
            elif ee < n_experts - 1:
                w1tok, w1par = prefetch_w1(ee + 1, 0, g)
            elif g == 0 and NG > 1:
                w1tok, w1par = prefetch_w1(0, 0, 1)
            else:
                w1tok, w1par = None, None
            r.wait("t", cur_tok)
            r.wait("t", xn2_v)
            for m in range(FC):
                for nt in range(2):
                    b_ = 4 + (m * 2 + nt) % 2
                    r.wait("t", bank_free[b_])
                    for kk in range(FC):
                        lhs = w1bufs[cur_par][:, kk * 1024 + m * 128:
                                              kk * 1024 + (m + 1) * 128]
                        rhs = xn2_b[:, kk * GT + nt * 512:
                                    kk * GT + nt * 512 + 512]
                        r.emit("t", lambda e, l=lhs, rr=rhs, b_=b_, kk=kk:
                               e.matmul(bank(b_), l, rr, start=(kk == 0),
                                        stop=(kk == FC - 1)))
                    mm_t = r.token("t")
                    fo = ms * 8 + m
                    r.wait("s", mm_t)
                    r.emit("s", lambda e, fo=fo, nt=nt, b_=b_, ee=ee:
                           e.activation(
                               h_b[:, fo * 1024 + nt * 512:
                                   fo * 1024 + nt * 512 + 512],
                               bank(b_), AF.Gelu,
                               bias=b1_c[:, ee * 32 + fo:ee * 32 + fo + 1]))
                    bank_free[b_] = r.token("s")
            w1_free[cur_par] = r.token("s")

        emit_A_ms(0, 0)
        for ms in range(1, 4):
            emit_A_ms(0, ms)
        h_done_s = r.token("s")
        a_mm_tok = r.token("t")

        for ee in range(n_experts):
            if w2tok is None:
                w2tok, w2par = prefetch_w2(ee, 0, g, 0)
            for th in range(2):
                for kb in range(16):
                    cur2_tok, cur2_par = w2tok, w2par
                    if kb < 15:
                        w2tok, w2par = prefetch_w2(ee, kb + 1, g, th)
                    elif th == 0:
                        w2tok, w2par = prefetch_w2(ee, 0, g, 1)
                    elif ee < n_experts - 1:
                        w2tok, w2par = prefetch_w2(ee + 1, 0, g, 0)
                    elif g == 0 and NG > 1:
                        w2tok, w2par = prefetch_w2(0, 0, 1, 0)
                    else:
                        w2tok, w2par = None, None
                    r.wait("t", cur2_tok)
                    if kb == 0:
                        r.wait("t", h_done_s)
                    for ks in range(2):
                        kk2 = kb * 2 + ks
                        for m2 in range(FC):
                            if kk2 == 0:
                                r.wait("t", bank_free[m2])
                            lhs = w2buf[:, cur2_par * 2048 + ks * 1024 + m2 * 128:
                                        cur2_par * 2048 + ks * 1024 + (m2 + 1) * 128]
                            rhs = h_b[:, kk2 * 1024 + th * 512:
                                      kk2 * 1024 + th * 512 + 512]
                            r.emit("t", lambda e, l=lhs, rr=rhs, m2=m2, kk2=kk2:
                                   e.matmul(bank(m2), l, rr, start=(kk2 == 0),
                                            stop=(kk2 == 31)))
                    w2_free[cur2_par] = r.token("t")
                bmm_t = r.token("t")
                r.wait("v", bmm_t)
                r.wait("v", res_read_tok)
                for m2 in range(FC):
                    sl = slice(m2 * GT + th * 512, m2 * GT + th * 512 + 512)
                    r.emit("v", lambda e, sl=sl, m2=m2, ee=ee:
                           e.scalar_tensor_tensor(
                               res_b[:, sl], bank(m2),
                               b2_c[:, ee * 8 + m2:ee * 8 + m2 + 1],
                               xn2_b[:, sl], ALU.add, ALU.add))
                    bank_free[m2] = r.token("v")
            res_v = r.token("v")

            last_in_group = ee == n_experts - 1
            if not last_in_group:
                emit_A_ms(ee + 1, 0)

            r.wait("t", gates_v)
            r.wait("t", bank_free[6])
            r.wait("t", bank_free[7])
            for tc in range(FC):
                src2 = gates_tok[:, tc * 8 + ee: tc * 8 + ee + 1]
                off = 6 * 512 + tc * 128
                r.emit("t", lambda e, src2=src2, off=off: e.transpose(
                    pall[0:1, off:off + 128], src2, ident[:]))
            gr_t = r.token("t")
            prev_gr_t = gr_t
            r.wait("v", gr_t)
            r.emit("v", lambda e: e.tensor_copy(
                g_r[0:1, :], pall[0:1, 6 * 512: 6 * 512 + 1024]))
            g_v = r.token("v")
            bank_free[6] = g_v
            bank_free[7] = g_v

            for b_ in range(4):
                r.wait("t", bank_free[b_])
            r.wait("t", res_v)
            for nt in range(2):
                for fc in range(FC):
                    rhs = res_b[:, fc * GT + nt * 512: fc * GT + nt * 512 + 512]
                    r.emit("t", lambda e, rr=rhs, fc=fc, nt=nt: e.matmul(
                        pall[0:1, nt * 512: nt * 512 + 512],
                        ones_col_b[:], rr, start=(fc == 0), stop=(fc == FC - 1)))
            for half in range(4):
                r.wait("s", res_v)
                r.wait("s", prev_est_t)
                if half > 0:
                    r.wait("s", r.token("t"))
                r.emit("s", lambda e, half=half: e.activation(
                    sq_b[:], res_b[:, half * 2 * GT:(half + 1) * 2 * GT],
                    AF.Square))
                sqh_s = r.token("s")
                r.wait("t", sqh_s)
                for nt in range(2):
                    for fc in range(2):
                        rhs = sq_b[:, fc * GT + nt * 512: fc * GT + nt * 512 + 512]
                        fcg = half * 2 + fc
                        r.emit("t", lambda e, rr=rhs, fcg=fcg, nt=nt: e.matmul(
                            pall[0:1, 1024 + nt * 512: 1024 + nt * 512 + 512],
                            ones_col_b[:], rr,
                            start=(fcg == 0), stop=(fcg == FC - 1)))
            est_t = r.token("t")
            prev_est_t = est_t

            if not last_in_group:
                emit_A_ms(ee + 1, 1)

            rowsdone = rows_combine(pall[0:1, 0:1024], pall[0:1, 1024:2048],
                                    EPS2, False, deps=[est_t, prev_ebc_t])
            for b_ in range(4):
                bank_free[b_] = rowsdone[0]
            r.wait("v", g_v)
            r.emit("v", lambda e: e.tensor_tensor(
                u1_rb[0:1, :], rs_r[0:1, :], g_r[0:1, :], ALU.mult))
            r.emit("v", lambda e: e.tensor_tensor(
                u2_rb[0:1, :], nm_r[0:1, :], g_r[0:1, :], ALU.mult))
            r.emit("v", lambda e: e.tensor_copy(g_rb[0:1, :], g_r[0:1, :]))
            urows_v = r.token("v")

            r.wait("t", urows_v)
            for nt in range(2):
                for b0, row in ((0, u1_rb), (2, u2_rb), (6, g_rb)):
                    r.wait("t", bank_free[b0 + nt])
                    r.emit("t", lambda e, b0=b0, nt=nt, row=row: e.matmul(
                        bank(b0 + nt), ones_row_b[:],
                        row[0:1, nt * 512:(nt + 1) * 512],
                        start=True, stop=True))
            ebc_t = r.token("t")
            prev_ebc_t = ebc_t

            r.wait("v", ebc_t)
            for fc in range(FC):
                for nt in range(2):
                    sl = slice(fc * GT + nt * 512, fc * GT + nt * 512 + 512)
                    r.emit("v", lambda e, sl=sl, nt=nt: e.tensor_tensor(
                        tscr[:, 0:512], res_b[:, sl], bank(0 + nt), ALU.mult))
                    r.emit("v", lambda e, nt=nt: e.tensor_tensor(
                        tscr[:, 0:512], tscr[:, 0:512], bank(2 + nt), ALU.add))
                    r.emit("v", lambda e, sl=sl, fc=fc, ee=ee:
                           e.scalar_tensor_tensor(
                               yacc[:, sl], tscr[:, 0:512],
                               fgam_c[:, ee * 8 + fc:ee * 8 + fc + 1],
                               yacc[:, sl], ALU.mult, ALU.add))
                    r.emit("v", lambda e, sl=sl, nt=nt, fc=fc, ee=ee:
                           e.scalar_tensor_tensor(
                               yacc[:, sl], bank(6 + nt),
                               fbet_c[:, ee * 8 + fc:ee * 8 + fc + 1],
                               yacc[:, sl], ALU.mult, ALU.add))
            y_v = r.token("v")
            res_read_tok = y_v
            prev_y_v = y_v
            for b_ in (0, 1, 2, 3, 6, 7):
                bank_free[b_] = y_v

            if not last_in_group:
                for ms in range(2, 4):
                    emit_A_ms(ee + 1, ms)
                h_done_s = r.token("s")
                a_mm_tok = r.token("t")
            tap(f"res_b{g}_{ee}", res_b[:], res_v)
            tap(f"g_r{g}_{ee}", g_r[0:1, :], urows_v)
            tap(f"rs_r{g}_{ee}", rs_r[0:1, :], urows_v)
            tap(f"nm_r{g}_{ee}", nm_r[0:1, :], urows_v)
            tap(f"h{g}_{ee}", h_b[:, 0:4096], h_done_s)
            tap(f"yacc{g}_{ee}", yacc[:], y_v)
        prev_a_mm_tok = a_mm_tok

        last_y = y_v
        ost_toks = []
        for tc in range(FC):
            par = tc % 2
            b0, b1_ = (0, 1) if par == 0 else (2, 3)
            r.wait("t", bank_free[b0])
            r.wait("t", bank_free[b1_])
            r.wait("t", last_y)
            if tc >= 2:
                r.wait("t", ost_toks[tc - 2])
            for fc in range(FC):
                src = yacc[:, fc * GT + tc * 128: fc * GT + (tc + 1) * 128]
                bb = b0 if fc < 4 else b1_
                off = bb * 512 + (fc % 4) * 128
                r.emit("t", lambda e, src=src, off=off: e.transpose(
                    pall[:, off:off + 128], src, ident[:]))
            otp_t = r.token("t")
            r.wait("v", otp_t)
            if tc >= 2:
                r.wait("v", ost_toks[tc - 2])
            r.emit("v", lambda e, par=par, b0=b0: e.tensor_copy(
                out_tok[par][:, 0:512], bank(b0)))
            r.emit("v", lambda e, par=par, b1_=b1_: e.tensor_copy(
                out_tok[par][:, 512:1024], bank(b1_)))
            ev_v = r.token("v")
            bank_free[b0] = ev_v
            bank_free[b1_] = ev_v
            r.wait("y", ev_v)
            stok = r.dma("y", f"ost{par}", lambda e, g=g, tc=tc, par=par: e.dma_start(
                out[g * GT + tc * 128: g * GT + (tc + 1) * 128, :],
                out_tok[par]))
            ost_toks.append(stok)
        prev_out_tp = r.token("t")
        prev_ost_all = [("ost0", r.dma_cnt.get("ost0", 0)),
                        ("ost1", r.dma_cnt.get("ost1", 0))]

    for sname in ("ost0", "ost1", "ost"):
        if sname in r.dma_cnt and r.dma_cnt[sname]:
            r.ops["y"].append(("wait", None, sname, r.dma_cnt[sname]))

    with nc.Block() as block:
        r.replay(block)
    ffn_scope.close()
    stack.close()
    nc.finalize()
    return nc


_NC_CACHE = [None]


def kernel(L0_attn_x, L0_attn_Wd, L0_attn_bd, L0_attn_Wi, L0_attn_bi,
           L0_attn_gamma, L0_attn_beta, L0_ffn_router_w, L0_ffn_w1, L0_ffn_b1,
           L0_ffn_w2, L0_ffn_b2, L0_ffn_gamma, L0_ffn_beta):
    x = np.ascontiguousarray(np.asarray(L0_attn_x, dtype=np.float32))
    B, T, _ = x.shape
    common = {
        "Wd": np.ascontiguousarray(np.asarray(L0_attn_Wd, np.float32)),
        "Wi": np.ascontiguousarray(np.asarray(L0_attn_Wi, np.float32)),
        "bd": np.ascontiguousarray(np.asarray(L0_attn_bd, np.float32)),
        "bi": np.ascontiguousarray(np.asarray(L0_attn_bi, np.float32)),
        "agam": np.ascontiguousarray(np.asarray(L0_attn_gamma, np.float32)),
        "abet": np.ascontiguousarray(np.asarray(L0_attn_beta, np.float32)),
        "rw": np.ascontiguousarray(np.asarray(L0_ffn_router_w, np.float32)),
        "w1": np.ascontiguousarray(np.asarray(L0_ffn_w1, np.float32)),
        "b1": np.ascontiguousarray(np.asarray(L0_ffn_b1, np.float32)),
        "w2": np.ascontiguousarray(np.asarray(L0_ffn_w2, np.float32)),
        "b2": np.ascontiguousarray(np.asarray(L0_ffn_b2, np.float32)),
        "fgam": np.ascontiguousarray(np.asarray(L0_ffn_gamma, np.float32)),
        "fbet": np.ascontiguousarray(np.asarray(L0_ffn_beta, np.float32)),
    }
    in_maps = []
    for c in range(8):
        b, h = c // 2, c % 2
        own = x[b, h * T_OWN:(h + 1) * T_OWN, :]
        prefix = x[b, 0:T_OWN, :] if h == 1 else own
        im = dict(common)
        im["ssm_x"] = np.ascontiguousarray(np.concatenate([prefix, own], axis=0))
        im["sel"] = np.full((128, 1), 1.0 if h == 1 else 0.0, dtype=np.float32)
        in_maps.append(im)

    if _NC_CACHE[0] is None:
        _NC_CACHE[0] = build_nc()
    nc = _NC_CACHE[0]
    res = run_bass_kernel_spmd(nc, in_maps, core_ids=list(range(8)))
    parts = [res.results[c]["out"] for c in range(8)]
    full = np.concatenate(parts, axis=0).reshape(B, T, D)
    return full


# version 11 (final)
# speedup vs baseline: 1.0714x; 1.0714x over previous
import numpy as np
import concourse.bass as bass
import concourse.mybir as mybir
from concourse.bass_utils import run_bass_kernel_spmd

F32 = mybir.dt.float32
BF16 = mybir.dt.bfloat16
AF = mybir.ActivationFunctionType
ALU = mybir.AluOpType
AX = mybir.AxisListType

D = 1024
FC = 8
FF = 4096
E = 8
T_ALL = 4096
T_OWN = 2048
ST = 512
N_SST = T_ALL // ST
GT = 1024
NG = T_OWN // GT
EPS1 = 1e-5
EPS2 = 1e-6


class Rec:

    ENGS = ("t", "v", "s", "q", "y")

    def __init__(self, nc, stack):
        self.nc = nc
        self.stack = stack
        self.ops = {e: [] for e in self.ENGS}
        self.cnt = {e: 0 for e in self.ENGS}
        self.sems = {}
        self.dma_cnt = {}
        self.last_wait = {}
        for e in self.ENGS:
            self._getsem("__" + e)

    def _getsem(self, name):
        if name not in self.sems:
            self.sems[name] = self.stack.enter_context(self.nc.semaphore(name))
            self.dma_cnt[name] = 0
        return self.sems[name]

    def emit(self, eng, fn):
        self.cnt[eng] += 1
        self.ops[eng].append(("op", fn, "__" + eng, 1))
        return ("__" + eng, self.cnt[eng])

    def dma(self, eng, sem_name, fn):
        self._getsem(sem_name)
        self.dma_cnt[sem_name] += 16
        self.ops[eng].append(("dma", fn, sem_name, 16))
        return (sem_name, self.dma_cnt[sem_name])

    def wait(self, eng, token):
        if token is None:
            return
        if isinstance(token, tuple) and len(token) == 2 and \
                isinstance(token[0], tuple):
            for t_ in token:
                self.wait(eng, t_)
            return
        sem_name, val = token
        if sem_name == "__" + eng:
            return
        key = (eng, sem_name)
        if self.last_wait.get(key, -1) >= val:
            return
        self.last_wait[key] = val
        self.ops[eng].append(("wait", None, sem_name, val))

    def token(self, eng):
        return ("__" + eng, self.cnt[eng])

    def sync_self(self, eng, token):
        if token is None:
            return
        sem_name, val = token
        key = (eng, sem_name)
        if self.last_wait.get(key, -1) >= val:
            return
        self.last_wait[key] = val
        self.ops[eng].append(("wait", None, sem_name, val))

    def replay(self, block):
        def mk(eng_key):
            ser = eng_key in ("v",)
            own = self.sems["__" + eng_key]

            def body(e):
                done = 0
                for kind, fn, name, val in self.ops[eng_key]:
                    if kind == "wait":
                        e.wait_ge(self.sems[name], val)
                    else:
                        if ser and kind == "op" and done > 0:
                            e.wait_ge(own, done)
                        fn(e).then_inc(self.sems[name], val)
                        if kind == "op":
                            done += val
            return body
        block.tensor(mk("t"))
        block.vector(mk("v"))
        block.scalar(mk("s"))
        block.gpsimd(mk("q"))
        block.sync(mk("y"))


def build_nc(n_tiles=N_SST, n_groups=NG, n_experts=E, taps=None):
    from contextlib import ExitStack
    nc = bass.Bass("TRN2", target_bir_lowering=False)

    ssm_x = nc.dram_tensor("ssm_x", [T_ALL, D], F32, kind="ExternalInput").ap()
    sel = nc.dram_tensor("sel", [128, 1], F32, kind="ExternalInput").ap()
    Wd = nc.dram_tensor("Wd", [D, D], F32, kind="ExternalInput").ap()
    Wi = nc.dram_tensor("Wi", [D, D], F32, kind="ExternalInput").ap()
    bd = nc.dram_tensor("bd", [D], F32, kind="ExternalInput").ap()
    bi = nc.dram_tensor("bi", [D], F32, kind="ExternalInput").ap()
    agam = nc.dram_tensor("agam", [D], F32, kind="ExternalInput").ap()
    abet = nc.dram_tensor("abet", [D], F32, kind="ExternalInput").ap()
    rw = nc.dram_tensor("rw", [D, E], F32, kind="ExternalInput").ap()
    w1 = nc.dram_tensor("w1", [E, D, FF], F32, kind="ExternalInput").ap()
    b1 = nc.dram_tensor("b1", [E, FF], F32, kind="ExternalInput").ap()
    w2 = nc.dram_tensor("w2", [E, FF, D], F32, kind="ExternalInput").ap()
    b2 = nc.dram_tensor("b2", [E, D], F32, kind="ExternalInput").ap()
    fgam = nc.dram_tensor("fgam", [E, D], F32, kind="ExternalInput").ap()
    fbet = nc.dram_tensor("fbet", [E, D], F32, kind="ExternalInput").ap()
    out = nc.dram_tensor("out", [T_OWN, D], F32, kind="ExternalOutput").ap()
    xmid_dram = nc.dram_tensor("xmid_scr", [128, FC, T_OWN], F32, kind="Internal").ap()
    w1s_dram = nc.dram_tensor("w1s_scr", [E, 4, 128, FC * 1024], BF16, kind="Internal").ap()
    w2s_dram = nc.dram_tensor("w2s_scr", [E, 16, 128, 2 * 1024], BF16, kind="Internal").ap()
    ident_dram = nc.inline_tensor(np.eye(128, dtype=np.float32), "identm").ap()

    stack = ExitStack()
    r = Rec(nc, stack)
    eg = lambda t_: stack.enter_context(t_)
    taps = taps or []
    tap_set = set(taps)

    def tap(name, ap, tok):
        if name not in tap_set:
            return
        dt_ = ap.dtype
        dr = nc.dram_tensor("tap_" + name, list(ap.shape), dt_,
                            kind="ExternalOutput").ap()
        r.wait("y", tok)
        ttok = r.dma("y", "ost", lambda e, dr=dr, ap=ap: e.dma_start(dr, ap))
        for engk in ("t", "v", "s", "q", "y"):
            r.wait(engk, ttok)

    ident = eg(nc.sbuf_tensor("ident", [128, 128], F32))
    ones_col_b = eg(nc.sbuf_tensor("ones_col_b", [128, 1], BF16))
    ones_col_f = eg(nc.sbuf_tensor("ones_col_f", [128, 1], F32))
    ones_row_b = eg(nc.sbuf_tensor("ones_row_b", [1, 128], BF16))
    sel_c = eg(nc.sbuf_tensor("sel_c", [128, 1], F32))
    bd_c = eg(nc.sbuf_tensor("bd_c", [128, FC], F32))
    bi_c = eg(nc.sbuf_tensor("bi_c", [128, FC], F32))
    agam_c = eg(nc.sbuf_tensor("agam_c", [128, FC], F32))
    abet_c = eg(nc.sbuf_tensor("abet_c", [128, FC], F32))
    rw_b = eg(nc.sbuf_tensor("rw_b", [128, FC * E], BF16))
    b1_c = eg(nc.sbuf_tensor("b1_c", [128, E * 32], F32))
    b2_c = eg(nc.sbuf_tensor("b2_c", [128, E * FC], F32))
    fgam_c = eg(nc.sbuf_tensor("fgam_c", [128, E * FC], F32))
    fbet_c = eg(nc.sbuf_tensor("fbet_c", [128, E * FC], F32))
    eps_c = eg(nc.sbuf_tensor("eps_c", [1, 2], F32))
    rs_r = eg(nc.sbuf_tensor("rs_r", [1, GT], F32))
    nm_r = eg(nc.sbuf_tensor("nm_r", [1, GT], F32))
    rs_rb = eg(nc.sbuf_tensor("rs_rb", [1, GT], BF16))
    nm_rb = eg(nc.sbuf_tensor("nm_rb", [1, GT], BF16))
    g_r = eg(nc.sbuf_tensor("g_r", [1, GT], F32))
    u1_rb = eg(nc.sbuf_tensor("u1_rb", [1, GT], BF16))
    u2_rb = eg(nc.sbuf_tensor("u2_rb", [1, GT], BF16))
    g_rb = eg(nc.sbuf_tensor("g_rb", [1, GT], BF16))
    colscr = eg(nc.sbuf_tensor("colscr", [128, 32], F32))
    carry = eg(nc.sbuf_tensor("carry", [128, FC], F32))
    gates_tok = eg(nc.sbuf_tensor("gates_tok", [128, 64], F32))
    logit_tok = eg(nc.sbuf_tensor("logit_tok", [128, 64], F32))

    pall = eg(nc.psum_tensor("pall", [128, 4096], F32))

    def bank(i):
        return pall[:, i * 512:(i + 1) * 512]

    cl = "cst"
    r.dma("y", cl, lambda e: e.dma_start(ident[:], ident_dram))
    r.dma("y", cl, lambda e: e.dma_start(sel_c[:], sel))
    r.dma("y", cl, lambda e: e.dma_start(bd_c[:], bd.rearrange("(fc p) -> p fc", p=128), allow_slow_non_contiguous=True))
    r.dma("y", cl, lambda e: e.dma_start(bi_c[:], bi.rearrange("(fc p) -> p fc", p=128), allow_slow_non_contiguous=True))
    r.dma("y", cl, lambda e: e.dma_start(agam_c[:], agam.rearrange("(fc p) -> p fc", p=128), allow_slow_non_contiguous=True))
    r.dma("y", cl, lambda e: e.dma_start(abet_c[:], abet.rearrange("(fc p) -> p fc", p=128), allow_slow_non_contiguous=True))
    rwb_tok = r.dma("q", "cstq", lambda e: e.dma_start(
        rw_b[:].rearrange("p (fc e) -> p fc e", fc=FC),
        rw.rearrange("(fc p) e -> p fc e", p=128)))
    r.dma("y", cl, lambda e: e.dma_start(
        b1_c[:].rearrange("p (e m) -> p e m", e=E),
        b1.rearrange("e (m p) -> p e m", p=128),
        allow_slow_non_contiguous=True))
    r.dma("y", cl, lambda e: e.dma_start(
        b2_c[:].rearrange("p (e m) -> p e m", e=E),
        b2.rearrange("e (m p) -> p e m", p=128),
        allow_slow_non_contiguous=True))
    r.dma("y", cl, lambda e: e.dma_start(
        fgam_c[:].rearrange("p (e m) -> p e m", e=E),
        fgam.rearrange("e (m p) -> p e m", p=128),
        allow_slow_non_contiguous=True))
    r.dma("y", cl, lambda e: e.dma_start(
        fbet_c[:].rearrange("p (e m) -> p e m", e=E),
        fbet.rearrange("e (m p) -> p e m", p=128),
        allow_slow_non_contiguous=True))
    cst_tok = (cl, r.dma_cnt[cl])

    r.wait("v", cst_tok)
    r.emit("v", lambda e: e.memset(ones_col_b[:], 1.0 / D))
    r.emit("v", lambda e: e.memset(ones_col_f[:], 1.0 / D))
    r.emit("v", lambda e: e.memset(ones_row_b[:], 1.0))
    r.emit("v", lambda e: e.memset(carry[:], 0.0))
    r.emit("v", lambda e: e.memset(eps_c[0:1, 0:1], EPS1))
    r.emit("v", lambda e: e.memset(eps_c[0:1, 1:2], EPS2))
    ones_v = r.token("v")

    last_rows_v = [None]

    def rows_combine(ps_sum_row, ps_sq_row, eps, want_bf16, n=None,
                     rb_out=None, nb_out=None, deps=()):
        if n is None:
            n = ps_sum_row.shape[-1]
        if rb_out is None:
            rb_out = rs_rb
        if nb_out is None:
            nb_out = nm_rb
        for d_ in deps:
            r.wait("v", d_)
        r.wait("v", last_rows_v[0])
        r.emit("v", lambda e: e.tensor_copy(nm_r[0:1, 0:n], ps_sum_row))
        r.emit("v", lambda e: e.tensor_tensor(
            nm_r[0:1, 0:n], nm_r[0:1, 0:n], ps_sum_row, ALU.mult))
        r.emit("v", lambda e: e.tensor_tensor(
            rs_r[0:1, 0:n], ps_sq_row, nm_r[0:1, 0:n], ALU.subtract))
        vtok = r.token("v")
        r.wait("s", vtok)
        ei = 0 if eps == EPS1 else 1
        r.emit("s", lambda e, ei=ei: e.activation(
            rs_r[0:1, 0:n], rs_r[0:1, 0:n], AF.Sqrt, bias=eps_c[0:1, ei:ei + 1]))
        stok = r.token("s")
        r.wait("v", stok)
        r.emit("v", lambda e: e.reciprocal(rs_r[0:1, 0:n], rs_r[0:1, 0:n]))
        r.emit("v", lambda e: e.scalar_tensor_tensor(
            nm_r[0:1, 0:n], ps_sum_row, -1.0, rs_r[0:1, 0:n],
            ALU.mult, ALU.mult))
        rd_tok = r.token("v")
        if want_bf16:
            r.emit("v", lambda e: e.tensor_copy(rb_out[0:1, 0:n], rs_r[0:1, 0:n]))
            r.emit("v", lambda e: e.tensor_copy(nb_out[0:1, 0:n], nm_r[0:1, 0:n]))
        last_rows_v[0] = r.token("v")
        return rd_tok, last_rows_v[0]

    ssm_scope = ExitStack()
    es = lambda t_: ssm_scope.enter_context(t_)
    WdWi_b = es(nc.sbuf_tensor("WdWi_b", [128, 2 * FC * 1024], BF16))
    xt = [es(nc.sbuf_tensor(f"xt{i}", [128, 4 * 1024], F32)) for i in range(2)]
    x_f2 = [es(nc.sbuf_tensor(f"x_f{i}", [128, FC * ST], F32)) for i in range(2)]
    xn_b = es(nc.sbuf_tensor("xn_b", [128, FC * ST], BF16))
    dec = es(nc.sbuf_tensor("dec", [128, FC * ST], F32))
    vscr = es(nc.sbuf_tensor("vscr", [128, 512], F32))
    inj = es(nc.sbuf_tensor("inj", [128, FC * ST], F32))
    states = es(nc.sbuf_tensor("states", [128, FC * ST], F32))
    ps_rs_rb = es(nc.sbuf_tensor("ps_rs_rb", [1, ST], BF16))
    ps_nm_rb = es(nc.sbuf_tensor("ps_nm_rb", [1, ST], BF16))
    st_sq = es(nc.sbuf_tensor("st_sq", [128, FC * ST], BF16))
    st_sq2 = inj[:].bitcast(BF16)[:, 0:FC * ST]

    wsem = "wdi"
    r.dma("q", wsem, lambda e: e.dma_start(
        WdWi_b[:, 0:FC * 1024].rearrange("p (kk d) -> p kk d", kk=FC),
        Wd.rearrange("(kk p) d -> p kk d", p=128)))
    r.dma("q", wsem, lambda e: e.dma_start(
        WdWi_b[:, FC * 1024:].rearrange("p (kk d) -> p kk d", kk=FC),
        Wi.rearrange("(kk p) d -> p kk d", p=128)))
    wdi_tok = (wsem, r.dma_cnt[wsem])

    xt_tok = [None] * (N_SST + 2)
    xt_free = [None, None]
    bank_free = [None] * 8
    tk = {}
    xm_store = [None] * (N_SST // 2)
    dec_tok = [None] * FC
    inj_tok = [None] * FC

    def load_xt(k):
        par = k % 2
        r.wait("y", xt_free[par])
        xt_tok[k] = r.dma("y", f"xl{par}", lambda e, k=k, par=par: e.dma_start(
            xt[par][:].rearrange("p (j d) -> p j d", j=4),
            ssm_x[k * ST:(k + 1) * ST, :].rearrange("(j p) d -> p j d", p=128)))

    load_xt(0)
    load_xt(1)

    def tp_evac(k):
        par = k % 2
        xf = x_f2[par]
        r.wait("t", cst_tok)
        r.wait("t", ones_v)
        r.wait("t", xt_tok[k])
        for fc in range(FC):
            b_ = fc % 2
            r.wait("t", bank_free[b_])
            for j in range(4):
                src2 = xt[par][:, j * 1024 + fc * 128: j * 1024 + (fc + 1) * 128]
                dst = pall[:, b_ * 512 + j * 128: b_ * 512 + (j + 1) * 128]
                r.emit("t", lambda e, d=dst, s2=src2: e.transpose(d, s2, ident[:]))
            tp_t = r.token("t")
            r.wait("s", tp_t)
            if fc == 0:
                for t_ in tk.get(("xfg", par), []):
                    r.wait("s", t_)
            r.emit("s", lambda e, fc=fc, b_=b_, xf=xf: e.activation(
                xf[:, fc * ST:(fc + 1) * ST], bank(b_), AF.Copy))
            bank_free[b_] = r.token("s")
        tk[("evac", k)] = r.token("s")
        xt_free[par] = r.token("t")
        if k + 2 < n_tiles:
            load_xt(k + 2)

    def stats_sum(k):
        xf = x_f2[k % 2]
        r.wait("t", tk[("evac", k)])
        r.wait("t", bank_free[6])
        for fc in range(FC):
            rhs = xf[:, fc * ST:(fc + 1) * ST]
            r.emit("t", lambda e, rr=rhs, fc=fc: e.matmul(
                pall[0:1, 6 * 512:7 * 512], ones_col_f[:], rr,
                start=(fc == 0), stop=(fc == FC - 1)))
        tk[("summ", k)] = r.token("t")

    def square(k):
        xf = x_f2[k % 2]
        r.wait("s", tk.get(("sqmm", k - 1)))
        r.sync_self("s", tk[("evac", k)])
        r.emit("s", lambda e, xf=xf: e.activation(st_sq[:], xf[:], AF.Square))
        tk[("sq", k)] = r.token("s")

    def stats_sq(k):
        r.wait("t", tk[("sq", k)])
        r.wait("t", bank_free[7])
        for fc in range(FC):
            rhs = st_sq[:, fc * ST:(fc + 1) * ST]
            r.emit("t", lambda e, rr=rhs, fc=fc: e.matmul(
                pall[0:1, 7 * 512:8 * 512], ones_col_b[:], rr,
                start=(fc == 0), stop=(fc == FC - 1)))
        tk[("sqmm", k)] = r.token("t")

    def combine1(k):
        rd, done = rows_combine(pall[0:1, 6 * 512:7 * 512],
                                pall[0:1, 7 * 512:8 * 512], EPS1, True, ST,
                                deps=[tk[("summ", k)], tk[("sqmm", k)],
                                      tk.get(("bc1", k - 1))])
        bank_free[6] = rd
        bank_free[7] = rd
        tk[("comb", k)] = done

    def bc1(k):
        r.wait("t", tk[("comb", k)])
        r.wait("t", bank_free[6])
        r.emit("t", lambda e: e.matmul(
            bank(6), ones_row_b[:], rs_rb[0:1, 0:ST], start=True, stop=True))
        r.wait("t", bank_free[7])
        r.emit("t", lambda e: e.matmul(
            bank(7), ones_row_b[:], nm_rb[0:1, 0:ST], start=True, stop=True))
        tk[("bc1", k)] = r.token("t")

    def norm(k):
        own = k >= N_SST // 2
        par = k % 2
        xf = x_f2[par]
        r.wait("v", tk[("bc1", k)])
        r.wait("v", tk.get(("decinj", k - 1)))
        fc_tok = [None] * FC
        for fc in range(FC):
            sl = slice(fc * ST, (fc + 1) * ST)
            scr = dec[:, sl] if own else vscr[:]
            r.emit("v", lambda e, sl=sl, xf=xf, scr=scr: e.tensor_tensor(
                scr, xf[:, sl], bank(6), ALU.mult))
            r.emit("v", lambda e, sl=sl, scr=scr: e.tensor_tensor(
                xn_b[:, sl], scr, bank(7), ALU.add))
            fc_tok[fc] = r.token("v")
        v = r.token("v")
        tk[("norm", k)] = v
        bank_free[6] = v
        if not own:
            bank_free[7] = v
            tk[("xfg", par)] = [v, tk[("summ", k)]]
            return
        qtok = [None] * FC
        for fc in range(FC):
            sl = slice(fc * ST, (fc + 1) * ST)
            r.wait("q", fc_tok[fc])
            r.emit("q", lambda e, sl=sl, xf=xf: e.tensor_tensor(
                xf[:, sl], xf[:, sl], dec[:, sl], ALU.add))
            qtok[fc] = r.token("q")
            r.wait("v", qtok[fc])
            r.emit("v", lambda e, sl=sl, xf=xf: e.tensor_tensor(
                xf[:, sl], xf[:, sl], bank(7), ALU.add))
        tk[("wacc", k)] = qtok
        bank_free[7] = r.token("v")

    def decinj(k, mid=None, mid2=None):
        r.wait("t", wdi_tok)
        r.wait("t", tk[("norm", k)])
        mm_banks = (4, 5)
        bi_ = 0
        for which in range(2):
            for m in range(FC):
                b_ = mm_banks[bi_ % 2]
                bi_ += 1
                r.wait("t", bank_free[b_])
                for kk in range(FC):
                    lhs = WdWi_b[:, which * FC * 1024 + kk * 1024 + m * 128:
                                 which * FC * 1024 + kk * 1024 + (m + 1) * 128]
                    rhs = xn_b[:, kk * ST:(kk + 1) * ST]
                    r.emit("t", lambda e, l=lhs, rr=rhs, b_=b_, kk=kk: e.matmul(
                        bank(b_), l, rr, start=(kk == 0), stop=(kk == FC - 1)))
                mm_t = r.token("t")
                r.wait("s", mm_t)
                r.wait("s", tk.get(("scanfc", k - 1), [None] * FC)[m])
                if which == 0:
                    r.wait("s", tk.get(("wacc", k), [None] * FC)[m])
                if which == 1 and m == 0:
                    r.wait("s", tk.get(("postmm", k - 1)))
                if which == 0:
                    r.emit("s", lambda e, m=m, b_=b_: e.activation(
                        dec[:, m * ST:(m + 1) * ST], bank(b_), AF.Sigmoid,
                        bias=bd_c[:, m:m + 1]))
                    dec_tok[m] = r.token("s")
                else:
                    r.emit("s", lambda e, m=m, b_=b_: e.activation(
                        inj[:, m * ST:(m + 1) * ST], bank(b_), AF.Identity,
                        bias=bi_c[:, m:m + 1]))
                    inj_tok[m] = r.token("s")
                bank_free[b_] = r.token("s")
                if which == 1 and m == 3 and mid2 is not None:
                    mid2()
            if which == 0 and mid is not None:
                mid()
        tk[("decinj", k)] = r.token("t")

    def scans(k):
        if k == N_SST // 2:
            r.emit("v", lambda e: e.tensor_scalar(
                carry[:], carry[:], sel_c[:, 0:1], None, ALU.mult))
        r.wait("v", tk.get(("postmm", k - 1)))
        r.wait("v", tk.get(("psq", k - 1)))
        sfc = [None] * FC
        for fc in range(FC):
            r.wait("v", dec_tok[fc])
            r.wait("v", inj_tok[fc])
            r.emit("v", lambda e, fc=fc: e.tensor_tensor_scan(
                states[:, fc * ST:(fc + 1) * ST], dec[:, fc * ST:(fc + 1) * ST],
                inj[:, fc * ST:(fc + 1) * ST], carry[:, fc:fc + 1],
                ALU.mult, ALU.add))
            sfc[fc] = r.token("v")
        r.emit("v", lambda e: e.tensor_copy(carry[:], states[:, ST - 1::ST]))
        tk[("scanfc", k)] = sfc
        tk[("scanall", k)] = r.token("v")

    def post_stats(k):
        r.wait("s", tk[("scanall", k)])
        r.emit("s", lambda e: e.activation(st_sq2, states[:], AF.Square))
        tk[("psq", k)] = r.token("s")
        r.wait("t", bank_free[2])
        r.wait("t", tk[("scanall", k)])
        for fc in range(FC):
            rhs = states[:, fc * ST:(fc + 1) * ST]
            r.emit("t", lambda e, rr=rhs, fc=fc: e.matmul(
                pall[0:1, 2 * 512:3 * 512], ones_col_f[:], rr,
                start=(fc == 0), stop=(fc == FC - 1)))
        r.wait("t", bank_free[3])
        r.wait("t", tk[("psq", k)])
        for fc in range(FC):
            rhs = st_sq2[:, fc * ST:(fc + 1) * ST]
            r.emit("t", lambda e, rr=rhs, fc=fc: e.matmul(
                pall[0:1, 3 * 512:4 * 512], ones_col_b[:], rr,
                start=(fc == 0), stop=(fc == FC - 1)))
        tk[("postmm", k)] = r.token("t")

    def post_combine(k):
        rd, done = rows_combine(pall[0:1, 2 * 512:3 * 512],
                                pall[0:1, 3 * 512:4 * 512], EPS1, True, ST,
                                rb_out=ps_rs_rb, nb_out=ps_nm_rb,
                                deps=[tk[("postmm", k)], tk.get(("bc2", k - 1))])
        bank_free[2] = rd
        bank_free[3] = rd
        tk[("pcomb", k)] = done

    def bc2(k):
        r.wait("t", tk[("pcomb", k)])
        r.wait("t", bank_free[2])
        r.emit("t", lambda e: e.matmul(
            bank(2), ones_row_b[:], ps_rs_rb[0:1, 0:ST], start=True, stop=True))
        r.wait("t", bank_free[3])
        r.emit("t", lambda e: e.matmul(
            bank(3), ones_row_b[:], ps_nm_rb[0:1, 0:ST], start=True, stop=True))
        tk[("bc2", k)] = r.token("t")

    def apply_store(k):
        par = k % 2
        xf = x_f2[par]
        r.wait("v", tk[("bc2", k)])
        for fc in range(FC):
            sl = slice(fc * ST, (fc + 1) * ST)
            r.emit("v", lambda e, sl=sl: e.tensor_tensor(
                vscr[:], states[:, sl], bank(2), ALU.mult))
            r.emit("v", lambda e: e.tensor_tensor(
                vscr[:], vscr[:], bank(3), ALU.add))
            r.emit("v", lambda e, sl=sl, fc=fc, xf=xf: e.scalar_tensor_tensor(
                xf[:, sl], vscr[:], agam_c[:, fc:fc + 1], xf[:, sl],
                ALU.mult, ALU.add))
            sttv = r.token("v")
            r.wait("s", sttv)
            r.emit("s", lambda e, sl=sl, fc=fc, xf=xf: e.activation(
                xf[:, sl], xf[:, sl], AF.Identity, bias=abet_c[:, fc:fc + 1]))
        apply_v = r.token("v")
        beta_s = r.token("s")
        bank_free[2] = apply_v
        bank_free[3] = apply_v
        tap(f"xmid{k}", xf[:], beta_s)
        kk_own = k - N_SST // 2
        r.wait("y", beta_s)
        st = r.dma("y", f"xms{kk_own % 2}", lambda e, kk_own=kk_own, xf=xf:
                   e.dma_start(xmid_dram[:, :, kk_own * ST:(kk_own + 1) * ST],
                               xf[:].rearrange("p (fc t) -> p fc t", fc=FC)))
        xm_store[kk_own] = st
        tk[("xfg", par)] = [st]

    tp_evac(0)
    stats_sum(0)
    square(0)
    stats_sq(0)
    combine1(0)
    bc1(0)
    norm(0)
    for k in range(n_tiles):
        own = k >= N_SST // 2
        if not own:
            if k + 1 < n_tiles:
                tp_evac(k + 1)
                stats_sum(k + 1)
                square(k + 1)
                stats_sq(k + 1)
                decinj(k, mid=lambda k=k: combine1(k + 1),
                       mid2=lambda k=k: bc1(k + 1))
                if k + 1 >= N_SST // 2:
                    scans(k)
                    norm(k + 1)
                else:
                    norm(k + 1)
                    scans(k)
            else:
                decinj(k)
                scans(k)
        else:
            decinj(k)
            scans(k)
            post_stats(k)
            if k + 1 < n_tiles:
                tp_evac(k + 1)
                stats_sum(k + 1)
                square(k + 1)
                stats_sq(k + 1)
            post_combine(k)
            if k + 1 < n_tiles:
                combine1(k + 1)
            bc2(k)
            if k + 1 < n_tiles:
                bc1(k + 1)
                norm(k + 1)
            apply_store(k)
        tap(f"xn_b{k}", xn_b[:], tk[("decinj", k)])
        tap(f"states{k}", states[:], tk[("scanall", k)])

    xms_all = [("xms0", r.dma_cnt.get("xms0", 0)), ("xms1", r.dma_cnt.get("xms1", 0))]
    ssm_t_done = r.token("t")
    ssm_v_done = r.token("v")
    ssm_s_done = r.token("s")
    ssm_scope.close()

    ffn_scope = ExitStack()
    ef = lambda t_: ffn_scope.enter_context(t_)
    h_b = ef(nc.sbuf_tensor("h_b", [128, 32 * 1024], BF16))
    xn2_b = ef(nc.sbuf_tensor("xn2_b", [128, FC * GT], BF16))
    yacc = ef(nc.sbuf_tensor("yacc", [128, FC * GT], F32))
    res_b = ef(nc.sbuf_tensor("res_b", [128, FC * GT], BF16))
    sq_b = ef(nc.sbuf_tensor("sq_b", [128, FC * GT // 4], BF16))
    w1buf0 = ef(nc.sbuf_tensor("w1buf0", [128, FC * 1024], BF16))
    w1buf1 = ef(nc.sbuf_tensor("w1buf1", [128, FC * 1024], BF16))
    w2buf = ef(nc.sbuf_tensor("w2buf", [128, 3 * 2048], BF16))
    tscr = sq_b[:].bitcast(F32)[:, 0:GT]
    out_tok = [h_b[:].bitcast(F32)[:, i * D:(i + 1) * D] for i in range(2)]
    w1bufs = (w1buf0, w1buf1)

    for engk in ("t", "v", "s", "y", "q"):
        r.wait(engk, ssm_v_done)
        r.wait(engk, ssm_s_done)
        r.wait(engk, ssm_t_done)

    w1_round = [0]
    w1_free = [None, None]
    w1_slot_store = [None, None]
    w2_slot_store = [None, None, None]
    w1s_tok = {}
    w2s_tok = {}

    def prefetch_w1(ee, ms, g):
        par = w1_round[0] % 2
        r.wait("q", w1_free[par])
        r.wait("y", w1_free[par])
        r.wait("q", w1_slot_store[par])
        r.wait("y", w1_slot_store[par])
        if g == 0:
            tok = r.dma("q", f"w1{par}", lambda e, ee=ee, ms=ms, par=par: e.dma_start(
                w1bufs[par][:].rearrange("p (kk f) -> p kk f", kk=FC),
                w1[ee, :, ms * 1024:(ms + 1) * 1024].rearrange(
                    "(kk p) f -> p kk f", p=128)))
            r.wait("y", tok)
            stok = r.dma("y", f"w1s{par}", lambda e, ee=ee, ms=ms, par=par:
                         e.dma_start(w1s_dram[ee, ms], w1bufs[par][:]))
            w1s_tok[(ee, ms)] = stok
            w1_slot_store[par] = stok
        else:
            r.wait("y", w1s_tok[(ee, ms)])
            tok = r.dma("y", f"w1h{par}", lambda e, ee=ee, ms=ms, par=par:
                        e.dma_start(w1bufs[par][:], w1s_dram[ee, ms]))
        w1_round[0] += 1
        return tok, par

    w2_round = [0]
    w2_free = [None, None, None]

    def prefetch_w2(ee, kb, g, th):
        par = w2_round[0] % 3
        r.wait("q", w2_free[par])
        r.wait("y", w2_free[par])
        r.wait("q", w2_slot_store[par])
        r.wait("y", w2_slot_store[par])
        dst = w2buf[:, par * 2048:(par + 1) * 2048]
        if g == 0 and th == 0:
            tok = r.dma("q", f"w2{par}", lambda e, ee=ee, kb=kb, dst=dst: e.dma_start(
                dst.rearrange("p (kk d) -> p kk d", kk=2),
                w2[ee, kb * 256:(kb + 1) * 256, :].rearrange(
                    "(kk p) d -> p kk d", p=128)))
            r.wait("y", tok)
            stok = r.dma("y", f"w2s{par}", lambda e, ee=ee, kb=kb, dst=dst:
                         e.dma_start(w2s_dram[ee, kb], dst))
            w2s_tok[(ee, kb)] = stok
            w2_slot_store[par] = stok
        else:
            r.wait("y", w2s_tok[(ee, kb)])
            tok = r.dma("y", f"w2h{par}", lambda e, ee=ee, kb=kb, dst=dst:
                        e.dma_start(dst, w2s_dram[ee, kb]))
        w2_round[0] += 1
        return tok, par


    prev_a_mm_tok = None
    prev_ost_all = None
    prev_out_tp = None
    prev_gr_t = None
    res_read_tok = None
    prev_est_t = None
    prev_ebc_t = None
    w1tok = w1par = None
    w2tok = w2par = None

    for g in range(n_groups):
        for t_ in xms_all:
            r.wait("y", t_)
        r.wait("y", prev_out_tp)
        yld = r.dma("y", "yld", lambda e, g=g: e.dma_start(
            yacc[:].rearrange("p (fc t) -> p fc t", fc=FC),
            xmid_dram[:, :, g * GT:(g + 1) * GT]))

        r.wait("v", yld)
        r.wait("v", prev_est_t)
        if prev_ost_all is not None:
            for t_ in prev_ost_all:
                r.wait("s", t_)
        r.emit("v", lambda e: e.tensor_copy(res_b[:], yacc[:]))
        xmb_v = r.token("v")
        for b_ in range(4):
            r.wait("t", bank_free[b_])
        r.wait("t", xmb_v)
        for nt in range(2):
            for fc in range(FC):
                rhs = res_b[:, fc * GT + nt * 512: fc * GT + nt * 512 + 512]
                r.emit("t", lambda e, rr=rhs, fc=fc, nt=nt: e.matmul(
                    pall[0:1, nt * 512: nt * 512 + 512],
                    ones_col_b[:], rr, start=(fc == 0), stop=(fc == FC - 1)))
        for half in range(4):
            r.wait("s", xmb_v)
            if half > 0:
                r.wait("s", r.token("t"))
            r.emit("s", lambda e, half=half: e.activation(
                sq_b[:], res_b[:, half * 2 * GT:(half + 1) * 2 * GT], AF.Square))
            sqh_s = r.token("s")
            r.wait("t", sqh_s)
            for nt in range(2):
                for fc in range(2):
                    rhs = sq_b[:, fc * GT + nt * 512: fc * GT + nt * 512 + 512]
                    fcg = half * 2 + fc
                    r.emit("t", lambda e, rr=rhs, fcg=fcg, nt=nt: e.matmul(
                        pall[0:1, 1024 + nt * 512: 1024 + nt * 512 + 512],
                        ones_col_b[:], rr,
                        start=(fcg == 0), stop=(fcg == FC - 1)))
        ln2_mm_t = r.token("t")
        ps_sum_row = pall[0:1, 0:1024]
        ps_sq_row = pall[0:1, 1024:2048]
        rowsdone = rows_combine(ps_sum_row, ps_sq_row, EPS1, True,
                                deps=[ln2_mm_t])
        for b_ in range(4):
            bank_free[b_] = rowsdone[0]
        rows2_v = rowsdone[1]
        r.wait("t", rows2_v)
        for nt in range(2):
            r.wait("t", bank_free[nt])
            r.emit("t", lambda e, nt=nt: e.matmul(
                bank(nt), ones_row_b[:], rs_rb[0:1, nt * 512:(nt + 1) * 512],
                start=True, stop=True))
            r.wait("t", bank_free[2 + nt])
            r.emit("t", lambda e, nt=nt: e.matmul(
                bank(2 + nt), ones_row_b[:], nm_rb[0:1, nt * 512:(nt + 1) * 512],
                start=True, stop=True))
        ln2_bc_t = r.token("t")
        r.wait("v", ln2_bc_t)
        r.wait("v", prev_a_mm_tok)
        for fc in range(FC):
            for nt in range(2):
                sl = slice(fc * GT + nt * 512, fc * GT + nt * 512 + 512)
                r.emit("v", lambda e, sl=sl, nt=nt: e.tensor_tensor(
                    tscr[:, 0:512], yacc[:, sl], bank(nt), ALU.mult))
                r.emit("v", lambda e, sl=sl, nt=nt: e.tensor_tensor(
                    xn2_b[:, sl], tscr[:, 0:512], bank(2 + nt), ALU.add))
        xn2_v = r.token("v")
        for b_ in range(4):
            bank_free[b_] = xn2_v

        rt_bank = 4
        r.wait("t", xn2_v)
        r.wait("t", rwb_tok)
        r.wait("t", bank_free[rt_bank])
        for tc in range(FC):
            for fc in range(FC):
                lhs = xn2_b[:, fc * GT + tc * 128: fc * GT + (tc + 1) * 128]
                r.emit("t", lambda e, l=lhs, tc=tc, fc=fc, rt_bank=rt_bank: e.matmul(
                    pall[:, rt_bank * 512 + tc * 8: rt_bank * 512 + (tc + 1) * 8],
                    l, rw_b[:, fc * E:(fc + 1) * E],
                    start=(fc == 0), stop=(fc == FC - 1)))
        rt_t = r.token("t")
        r.wait("v", rt_t)
        r.wait("v", prev_gr_t)
        r.emit("v", lambda e, rt_bank=rt_bank: e.tensor_copy(
            logit_tok[:], pall[:, rt_bank * 512: rt_bank * 512 + 64]))
        bank_free[rt_bank] = r.token("v")
        for tc in range(FC):
            lt = logit_tok[:, tc * 8:(tc + 1) * 8]
            r.emit("v", lambda e, lt=lt: e.tensor_tensor(
                colscr[:, 4:8], lt[:, 0:4], lt[:, 4:8], ALU.max))
            r.emit("v", lambda e: e.tensor_tensor(
                colscr[:, 8:10], colscr[:, 4:6], colscr[:, 6:8], ALU.max))
            r.emit("v", lambda e: e.tensor_tensor(
                colscr[:, 0:1], colscr[:, 8:9], colscr[:, 9:10], ALU.max))
            r.emit("v", lambda e, lt=lt: e.tensor_scalar(
                colscr[:, 4:12], lt, colscr[:, 0:1], None, ALU.is_ge))
            r.emit("v", lambda e, lt=lt: e.scalar_tensor_tensor(
                colscr[:, 16:24], colscr[:, 4:12], -1e30, lt, ALU.mult, ALU.add))
            r.emit("v", lambda e: e.tensor_tensor(
                colscr[:, 24:28], colscr[:, 16:20], colscr[:, 20:24], ALU.max))
            r.emit("v", lambda e: e.tensor_tensor(
                colscr[:, 28:30], colscr[:, 24:26], colscr[:, 26:28], ALU.max))
            r.emit("v", lambda e: e.tensor_tensor(
                colscr[:, 1:2], colscr[:, 28:29], colscr[:, 29:30], ALU.max))
            r.emit("v", lambda e: e.tensor_scalar(
                colscr[:, 16:24], colscr[:, 16:24], colscr[:, 1:2], None, ALU.is_ge))
            r.emit("v", lambda e: e.tensor_tensor(
                colscr[:, 2:3], colscr[:, 0:1], colscr[:, 1:2], ALU.subtract))
            r.emit("v", lambda e: e.tensor_scalar(
                colscr[:, 3:4], colscr[:, 2:3], -1.0, None, ALU.mult))
            sg_v = r.token("v")
            r.wait("s", sg_v)
            r.emit("s", lambda e: e.activation(
                colscr[:, 12:14], colscr[:, 2:4], AF.Sigmoid))
            sig_s = r.token("s")
            r.wait("v", sig_s)
            r.emit("v", lambda e, tc=tc: e.tensor_scalar(
                gates_tok[:, tc * 8:(tc + 1) * 8], colscr[:, 4:12],
                colscr[:, 12:13], None, ALU.mult))
            r.emit("v", lambda e, tc=tc: e.scalar_tensor_tensor(
                gates_tok[:, tc * 8:(tc + 1) * 8], colscr[:, 16:24],
                colscr[:, 13:14], gates_tok[:, tc * 8:(tc + 1) * 8],
                ALU.mult, ALU.add))
        gates_v = r.token("v")

        if w1tok is None:
            w1tok, w1par = prefetch_w1(0, 0, g)
        prev_y_v = None
        tap(f"yld{g}", yacc[:], ("__v", r.cnt["v"]))
        tap(f"xn2_b{g}", xn2_b[:], xn2_v)
        tap(f"logit{g}", logit_tok[:], gates_v)
        tap(f"gates{g}", gates_tok[:], gates_v)

        def emit_A_ms(ee, ms):
            nonlocal w1tok, w1par
            cur_tok, cur_par = w1tok, w1par
            if ms < 3:
                w1tok, w1par = prefetch_w1(ee, ms + 1, g)
            elif ee < n_experts - 1:
                w1tok, w1par = prefetch_w1(ee + 1, 0, g)
            elif g == 0 and NG > 1:
                w1tok, w1par = prefetch_w1(0, 0, 1)
            else:
                w1tok, w1par = None, None
            r.wait("t", cur_tok)
            r.wait("t", xn2_v)
            for m in range(FC):
                for nt in range(2):
                    b_ = 4 + (m * 2 + nt) % 2
                    r.wait("t", bank_free[b_])
                    for kk in range(FC):
                        lhs = w1bufs[cur_par][:, kk * 1024 + m * 128:
                                              kk * 1024 + (m + 1) * 128]
                        rhs = xn2_b[:, kk * GT + nt * 512:
                                    kk * GT + nt * 512 + 512]
                        r.emit("t", lambda e, l=lhs, rr=rhs, b_=b_, kk=kk:
                               e.matmul(bank(b_), l, rr, start=(kk == 0),
                                        stop=(kk == FC - 1)))
                    mm_t = r.token("t")
                    fo = ms * 8 + m
                    r.wait("s", mm_t)
                    r.emit("s", lambda e, fo=fo, nt=nt, b_=b_, ee=ee:
                           e.activation(
                               h_b[:, fo * 1024 + nt * 512:
                                   fo * 1024 + nt * 512 + 512],
                               bank(b_), AF.Gelu,
                               bias=b1_c[:, ee * 32 + fo:ee * 32 + fo + 1]))
                    bank_free[b_] = r.token("s")
            w1_free[cur_par] = r.token("s")

        emit_A_ms(0, 0)
        for ms in range(1, 4):
            emit_A_ms(0, ms)
        h_done_s = r.token("s")
        a_mm_tok = r.token("t")

        for ee in range(n_experts):
            if w2tok is None:
                w2tok, w2par = prefetch_w2(ee, 0, g, 0)
            for th in range(2):
                for kb in range(16):
                    cur2_tok, cur2_par = w2tok, w2par
                    if kb < 15:
                        w2tok, w2par = prefetch_w2(ee, kb + 1, g, th)
                    elif th == 0:
                        w2tok, w2par = prefetch_w2(ee, 0, g, 1)
                    elif ee < n_experts - 1:
                        w2tok, w2par = prefetch_w2(ee + 1, 0, g, 0)
                    elif g == 0 and NG > 1:
                        w2tok, w2par = prefetch_w2(0, 0, 1, 0)
                    else:
                        w2tok, w2par = None, None
                    r.wait("t", cur2_tok)
                    if kb == 0:
                        r.wait("t", h_done_s)
                    for ks in range(2):
                        kk2 = kb * 2 + ks
                        for m2 in range(FC):
                            if kk2 == 0:
                                r.wait("t", bank_free[m2])
                            lhs = w2buf[:, cur2_par * 2048 + ks * 1024 + m2 * 128:
                                        cur2_par * 2048 + ks * 1024 + (m2 + 1) * 128]
                            rhs = h_b[:, kk2 * 1024 + th * 512:
                                      kk2 * 1024 + th * 512 + 512]
                            r.emit("t", lambda e, l=lhs, rr=rhs, m2=m2, kk2=kk2:
                                   e.matmul(bank(m2), l, rr, start=(kk2 == 0),
                                            stop=(kk2 == 31)))
                    w2_free[cur2_par] = r.token("t")
                bmm_t = r.token("t")
                r.wait("v", bmm_t)
                r.wait("v", res_read_tok)
                for m2 in range(FC):
                    sl = slice(m2 * GT + th * 512, m2 * GT + th * 512 + 512)
                    r.emit("v", lambda e, sl=sl, m2=m2, ee=ee:
                           e.scalar_tensor_tensor(
                               res_b[:, sl], bank(m2),
                               b2_c[:, ee * 8 + m2:ee * 8 + m2 + 1],
                               xn2_b[:, sl], ALU.add, ALU.add))
                    bank_free[m2] = r.token("v")
            res_v = r.token("v")

            last_in_group = ee == n_experts - 1
            if not last_in_group:
                emit_A_ms(ee + 1, 0)

            r.wait("t", gates_v)
            r.wait("t", bank_free[6])
            r.wait("t", bank_free[7])
            for tc in range(FC):
                src2 = gates_tok[:, tc * 8 + ee: tc * 8 + ee + 1]
                off = 6 * 512 + tc * 128
                r.emit("t", lambda e, src2=src2, off=off: e.transpose(
                    pall[0:1, off:off + 128], src2, ident[:]))
            gr_t = r.token("t")
            prev_gr_t = gr_t
            r.wait("v", gr_t)
            r.emit("v", lambda e: e.tensor_copy(
                g_r[0:1, :], pall[0:1, 6 * 512: 6 * 512 + 1024]))
            g_v = r.token("v")
            bank_free[6] = g_v
            bank_free[7] = g_v

            for b_ in range(4):
                r.wait("t", bank_free[b_])
            r.wait("t", res_v)
            for nt in range(2):
                for fc in range(FC):
                    rhs = res_b[:, fc * GT + nt * 512: fc * GT + nt * 512 + 512]
                    r.emit("t", lambda e, rr=rhs, fc=fc, nt=nt: e.matmul(
                        pall[0:1, nt * 512: nt * 512 + 512],
                        ones_col_b[:], rr, start=(fc == 0), stop=(fc == FC - 1)))
            for half in range(4):
                r.wait("s", res_v)
                r.wait("s", prev_est_t)
                if half > 0:
                    r.wait("s", r.token("t"))
                r.emit("s", lambda e, half=half: e.activation(
                    sq_b[:], res_b[:, half * 2 * GT:(half + 1) * 2 * GT],
                    AF.Square))
                sqh_s = r.token("s")
                r.wait("t", sqh_s)
                for nt in range(2):
                    for fc in range(2):
                        rhs = sq_b[:, fc * GT + nt * 512: fc * GT + nt * 512 + 512]
                        fcg = half * 2 + fc
                        r.emit("t", lambda e, rr=rhs, fcg=fcg, nt=nt: e.matmul(
                            pall[0:1, 1024 + nt * 512: 1024 + nt * 512 + 512],
                            ones_col_b[:], rr,
                            start=(fcg == 0), stop=(fcg == FC - 1)))
            est_t = r.token("t")
            prev_est_t = est_t

            if not last_in_group:
                emit_A_ms(ee + 1, 1)

            rowsdone = rows_combine(pall[0:1, 0:1024], pall[0:1, 1024:2048],
                                    EPS2, False, deps=[est_t, prev_ebc_t])
            for b_ in range(4):
                bank_free[b_] = rowsdone[0]
            r.wait("v", g_v)
            r.emit("v", lambda e: e.tensor_tensor(
                u1_rb[0:1, :], rs_r[0:1, :], g_r[0:1, :], ALU.mult))
            r.emit("v", lambda e: e.tensor_tensor(
                u2_rb[0:1, :], nm_r[0:1, :], g_r[0:1, :], ALU.mult))
            r.emit("v", lambda e: e.tensor_copy(g_rb[0:1, :], g_r[0:1, :]))
            urows_v = r.token("v")

            r.wait("t", urows_v)
            for nt in range(2):
                for b0, row in ((0, u1_rb), (2, u2_rb), (6, g_rb)):
                    r.wait("t", bank_free[b0 + nt])
                    r.emit("t", lambda e, b0=b0, nt=nt, row=row: e.matmul(
                        bank(b0 + nt), ones_row_b[:],
                        row[0:1, nt * 512:(nt + 1) * 512],
                        start=True, stop=True))
            ebc_t = r.token("t")
            prev_ebc_t = ebc_t

            r.wait("v", ebc_t)
            for fc in range(FC):
                for nt in range(2):
                    sl = slice(fc * GT + nt * 512, fc * GT + nt * 512 + 512)
                    r.emit("v", lambda e, sl=sl, nt=nt: e.tensor_tensor(
                        tscr[:, 0:512], res_b[:, sl], bank(0 + nt), ALU.mult))
                    r.emit("v", lambda e, nt=nt: e.tensor_tensor(
                        tscr[:, 0:512], tscr[:, 0:512], bank(2 + nt), ALU.add))
                    r.emit("v", lambda e, sl=sl, fc=fc, ee=ee:
                           e.scalar_tensor_tensor(
                               yacc[:, sl], tscr[:, 0:512],
                               fgam_c[:, ee * 8 + fc:ee * 8 + fc + 1],
                               yacc[:, sl], ALU.mult, ALU.add))
                    r.emit("v", lambda e, sl=sl, nt=nt, fc=fc, ee=ee:
                           e.scalar_tensor_tensor(
                               yacc[:, sl], bank(6 + nt),
                               fbet_c[:, ee * 8 + fc:ee * 8 + fc + 1],
                               yacc[:, sl], ALU.mult, ALU.add))
            y_v = r.token("v")
            res_read_tok = y_v
            prev_y_v = y_v
            for b_ in (0, 1, 2, 3, 6, 7):
                bank_free[b_] = y_v

            if not last_in_group:
                for ms in range(2, 4):
                    emit_A_ms(ee + 1, ms)
                h_done_s = r.token("s")
                a_mm_tok = r.token("t")
            tap(f"res_b{g}_{ee}", res_b[:], res_v)
            tap(f"g_r{g}_{ee}", g_r[0:1, :], urows_v)
            tap(f"rs_r{g}_{ee}", rs_r[0:1, :], urows_v)
            tap(f"nm_r{g}_{ee}", nm_r[0:1, :], urows_v)
            tap(f"h{g}_{ee}", h_b[:, 0:4096], h_done_s)
            tap(f"yacc{g}_{ee}", yacc[:], y_v)
        prev_a_mm_tok = a_mm_tok

        last_y = y_v
        ost_toks = []
        for tc in range(FC):
            par = tc % 2
            b0, b1_ = (0, 1) if par == 0 else (2, 3)
            r.wait("t", bank_free[b0])
            r.wait("t", bank_free[b1_])
            r.wait("t", last_y)
            if tc >= 2:
                r.wait("t", ost_toks[tc - 2])
            for fc in range(FC):
                src = yacc[:, fc * GT + tc * 128: fc * GT + (tc + 1) * 128]
                bb = b0 if fc < 4 else b1_
                off = bb * 512 + (fc % 4) * 128
                r.emit("t", lambda e, src=src, off=off: e.transpose(
                    pall[:, off:off + 128], src, ident[:]))
            otp_t = r.token("t")
            r.wait("v", otp_t)
            if tc >= 2:
                r.wait("v", ost_toks[tc - 2])
            r.emit("v", lambda e, par=par, b0=b0: e.tensor_copy(
                out_tok[par][:, 0:512], bank(b0)))
            r.emit("v", lambda e, par=par, b1_=b1_: e.tensor_copy(
                out_tok[par][:, 512:1024], bank(b1_)))
            ev_v = r.token("v")
            bank_free[b0] = ev_v
            bank_free[b1_] = ev_v
            r.wait("y", ev_v)
            stok = r.dma("y", f"ost{par}", lambda e, g=g, tc=tc, par=par: e.dma_start(
                out[g * GT + tc * 128: g * GT + (tc + 1) * 128, :],
                out_tok[par]))
            ost_toks.append(stok)
        prev_out_tp = r.token("t")
        prev_ost_all = [("ost0", r.dma_cnt.get("ost0", 0)),
                        ("ost1", r.dma_cnt.get("ost1", 0))]

    for sname in ("ost0", "ost1", "ost"):
        if sname in r.dma_cnt and r.dma_cnt[sname]:
            r.ops["y"].append(("wait", None, sname, r.dma_cnt[sname]))

    with nc.Block() as block:
        r.replay(block)
    ffn_scope.close()
    stack.close()
    nc.finalize()
    return nc


_NC_CACHE = [None]


def kernel(L0_attn_x, L0_attn_Wd, L0_attn_bd, L0_attn_Wi, L0_attn_bi,
           L0_attn_gamma, L0_attn_beta, L0_ffn_router_w, L0_ffn_w1, L0_ffn_b1,
           L0_ffn_w2, L0_ffn_b2, L0_ffn_gamma, L0_ffn_beta):
    x = np.ascontiguousarray(np.asarray(L0_attn_x, dtype=np.float32))
    B, T, _ = x.shape
    common = {
        "Wd": np.ascontiguousarray(np.asarray(L0_attn_Wd, np.float32)),
        "Wi": np.ascontiguousarray(np.asarray(L0_attn_Wi, np.float32)),
        "bd": np.ascontiguousarray(np.asarray(L0_attn_bd, np.float32)),
        "bi": np.ascontiguousarray(np.asarray(L0_attn_bi, np.float32)),
        "agam": np.ascontiguousarray(np.asarray(L0_attn_gamma, np.float32)),
        "abet": np.ascontiguousarray(np.asarray(L0_attn_beta, np.float32)),
        "rw": np.ascontiguousarray(np.asarray(L0_ffn_router_w, np.float32)),
        "w1": np.ascontiguousarray(np.asarray(L0_ffn_w1, np.float32)),
        "b1": np.ascontiguousarray(np.asarray(L0_ffn_b1, np.float32)),
        "w2": np.ascontiguousarray(np.asarray(L0_ffn_w2, np.float32)),
        "b2": np.ascontiguousarray(np.asarray(L0_ffn_b2, np.float32)),
        "fgam": np.ascontiguousarray(np.asarray(L0_ffn_gamma, np.float32)),
        "fbet": np.ascontiguousarray(np.asarray(L0_ffn_beta, np.float32)),
    }
    in_maps = []
    for c in range(8):
        b, h = c // 2, c % 2
        own = x[b, h * T_OWN:(h + 1) * T_OWN, :]
        prefix = x[b, 0:T_OWN, :] if h == 1 else own
        im = dict(common)
        im["ssm_x"] = np.ascontiguousarray(np.concatenate([prefix, own], axis=0))
        im["sel"] = np.full((128, 1), 1.0 if h == 1 else 0.0, dtype=np.float32)
        in_maps.append(im)

    if _NC_CACHE[0] is None:
        _NC_CACHE[0] = build_nc()
    nc = _NC_CACHE[0]
    res = run_bass_kernel_spmd(nc, in_maps, core_ids=list(range(8)))
    parts = [res.results[c]["out"] for c in range(8)]
    full = np.concatenate(parts, axis=0).reshape(B, T, D)
    return full
